# version 16
# baseline (speedup 1.0000x reference)
import numpy as np
import concourse.bass as bass
import concourse.mybir as mybir
from concourse.bass_utils import run_bass_kernel_spmd

F32 = mybir.dt.float32
BF16 = mybir.dt.bfloat16
AF = mybir.ActivationFunctionType
ALU = mybir.AluOpType
AX = mybir.AxisListType

NCORES = 8
FOLD_KIND = "sel"
T = 2048
NT = T // 128
D = 1024
FH = 4
NK = 64
DCD = 4096
NEXP = 4096
EPS = 1e-5
THR_MARGIN = 1.0 - 4e-6

ENGINES = ("pe", "act", "dve", "pool", "sp")


class Buf:
    __slots__ = ("name", "writers", "readers")

    def __init__(self, name):
        self.name = name
        self.writers = []
        self.readers = []


class Prog:
    def __init__(self):
        self.ops = []
        self._fence_deps = None

    def fence(self):
        last = {}
        for i, o in enumerate(self.ops):
            last[(o[0], o[3])] = i
        self._fence_deps = set(last.values())

    def add(self, engine, emit, reads, writes, dma, kind=""):
        dset = set()
        for b in reads:
            dset.update(b.writers)
        for b in writes:
            dset.update(b.writers)
            dset.update(b.readers)
        if self._fence_deps is not None:
            dset.update(self._fence_deps)
        oid = len(self.ops)
        dset.discard(oid)
        self.ops.append((engine, emit, dset, dma, kind))
        for b in reads:
            b.readers.append(oid)
        for b in writes:
            b.writers.append(oid)
            b.readers = []
        return oid

    def fence_buf(self, b):
        b.writers = []
        b.readers = []

    _COST = {"pe": 250, "act": 700, "dve": 1100, "pool": 900, "sp": 400}

    USE_LIST_SCHED = False

    def schedule(self):
        if not self.USE_LIST_SCHED:
            n = len(self.ops)
            return {e: [i for i in range(n) if self.ops[i][0] == e] for e in ENGINES}
        return self._schedule_list()

    def _schedule_list(self):
        n = len(self.ops)
        eng_of = [o[0] for o in self.ops]
        cost = [self._COST[e] for e in eng_of]
        dependents = [[] for _ in range(n)]
        ndeps = [0] * n
        for i, (_, _, deps, _, _) in enumerate(self.ops):
            ndeps[i] = len(deps)
            for d in deps:
                dependents[d].append(i)
        prio = [0] * n
        for i in range(n - 1, -1, -1):
            m = 0
            for j in dependents[i]:
                if prio[j] > m:
                    m = prio[j]
            prio[i] = m + cost[i]
        import heapq
        ready = {e: [] for e in ENGINES}
        ready_at = [0.0] * n
        done_t = [0.0] * n
        remaining = [len(self.ops[i][2]) for i in range(n)]
        for i in range(n):
            if remaining[i] == 0:
                heapq.heappush(ready[eng_of[i]], (-prio[i], i))
        clock = {e: 0.0 for e in ENGINES}
        order = {e: [] for e in ENGINES}
        total = 0
        while total < n:
            best = None
            for e in ENGINES:
                while ready[e]:
                    _, i = ready[e][0]
                    break
                if not ready[e]:
                    continue
                _, i = ready[e][0]
                start = max(clock[e], ready_at[i])
                key = (start, -prio[i])
                if best is None or key < best[0]:
                    best = (key, e)
            if best is None:
                raise RuntimeError("scheduler deadlock")
            _, e = best
            _, i = heapq.heappop(ready[e])
            start = max(clock[e], ready_at[i])
            fin = start + cost[i]
            clock[e] = fin
            done_t[i] = fin
            order[e].append(i)
            total += 1
            for j in dependents[i]:
                if done_t[i] > ready_at[j]:
                    ready_at[j] = done_t[i]
                remaining[j] -= 1
                if remaining[j] == 0:
                    heapq.heappush(ready[eng_of[j]], (-prio[j], j))
        return order

    def emit(self, sems):
        n = len(self.ops)
        eng_of = [o[0] for o in self.ops]
        dma_of = [o[3] for o in self.ops]
        DEEP = {"bn", "max8", "recip", "sel", "reduce"}
        deep = [(o[4] in DEEP) or (o[0] in ("dve", "act") and not o[3]) for o in self.ops]
        needs_inc = [o[3] for o in self.ops]
        for i, (eng, _, deps, dma, _k) in enumerate(self.ops):
            for d in deps:
                if eng_of[d] != eng or self.ops[d][3] or deep[d]:
                    needs_inc[d] = True
        per_engine = self.schedule()
        pos = [0] * n
        for e in ENGINES:
            for k, i in enumerate(per_engine[e]):
                pos[i] = k
        val_after = [0] * n
        cnt = {}
        for e in ENGINES:
            for i in per_engine[e]:
                key = (e, dma_of[i])
                if needs_inc[i]:
                    cnt[key] = cnt.get(key, 0) + (16 if dma_of[i] else 1)
                val_after[i] = cnt.get(key, 0)
        dma_prefix = {e: [0] * (len(per_engine[e]) + 1) for e in ENGINES}
        for e in ENGINES:
            for k, i in enumerate(per_engine[e]):
                dma_prefix[e][k + 1] = dma_prefix[e][k] + (16 if (dma_of[i] and needs_inc[i]) else 0)

        emitted = {}
        nosync = mybir.DependencyInfo(mybir.DependencyClass.NoSync, mybir.DependencyReason.DATA)

        def run_engine(handle, ename):
            waited = {}
            for i in per_engine[ename]:
                eng, emit, deps, dma, _k = self.ops[i]
                need = {}
                same_eng = []
                need_nop = False
                for d in deps:
                    key = (eng_of[d], dma_of[d])
                    if key[0] == ename and not key[1]:
                        same_eng.append(d)
                        if not deep[d]:
                            continue
                        if ename == "dve" and self.ops[d][4] == "":
                            if pos[d] == pos[i] - 1:
                                need_nop = True
                            continue
                    v = val_after[d]
                    if v > need.get(key, 0):
                        need[key] = v
                for key in sorted(need):
                    v = need[key]
                    if v > waited.get(key, 0):
                        handle.wait_ge(sems[key], v)
                        waited[key] = v
                if need_nop:
                    handle.engine_nop()
                inst = emit(handle)
                emitted[i] = inst
                if inst is not None:
                    for d in same_eng:
                        pi = emitted.get(d)
                        if pi is not None:
                            inst.ins.add_dependency(pi.ins.name, nosync)
                if needs_inc[i]:
                    if inst is None:
                        raise RuntimeError(f"op {i} on {ename} needs inc but returned None")
                    inst.then_inc(sems[(ename, dma)], 16 if dma else 1)

        return run_engine


def build_nc(dbg=False, max_ops=None):
    nc = bass.Bass("TRN2", target_bir_lowering=False, debug=False, detect_race_conditions=False)

    def din(name, shape, dtype=F32):
        return nc.dram_tensor(name, list(shape), dtype, kind="ExternalInput").ap()

    x_h = din("x", (T, D))
    qwT_h = din("qwT", (D, D))
    kwT_h = din("kwT", (D, D))
    vwT_h = din("vwT", (D, D))
    owT_h = din("owT", (D, D))
    fcT_h = din("fcT", (D, DCD))
    projT_h = din("projT", (DCD, D))
    wg_h = din("wg", (D, DCD))
    wu_h = din("wu", (D, DCD))
    wd_h = din("wd", (DCD, D))
    wq_h = din("wq", (D, 512))
    keysT_h = din("keysT", (FH, 2, NK, NK))
    downT_h = din("downT", (D, NEXP))
    up_h = din("up", (NEXP, D))
    qb_h = din("qb", (1, D))
    kb_h = din("kb", (1, D))
    vb_h = din("vb", (1, D))
    ob_h = din("ob", (1, D))
    projb_h = din("projb", (1, D))
    bd_h = din("bd", (1, D))
    fcb_h = din("fcb", (DCD,))
    bg_h = din("bg", (DCD,))
    bu_h = din("bu", (DCD,))
    ln1w_h = din("ln1w", (D,))
    ln1b_h = din("ln1b", (D,))
    ln2w_h = din("ln2w", (D,))
    ln2b_h = din("ln2b", (D,))
    y_h = nc.dram_tensor("y", [T, D], F32, kind="ExternalOutput").ap()
    wspill_h = nc.dram_tensor("wspill", [T, NEXP], BF16, kind="Internal").ap()
    dbg_h = {}
    if dbg:
        kk = "ExternalOutput" if dbg == 2 else "Internal"
        for nm in ("dbg_attn", "dbg_fc", "dbg_mlp"):
            dbg_h[nm] = nc.dram_tensor(nm, [T, D], F32, kind=kk).ap()
        dbg_h["dbg_f2"] = nc.dram_tensor("dbg_f2", [128, 8 * T], BF16, kind=kk).ap()
        dbg_h["dbg_f2a"] = nc.dram_tensor("dbg_f2a", [128, 8 * T], BF16, kind=kk).ap()
        dbg_h["dbg_hf"] = nc.dram_tensor("dbg_hf", [128, 8 * T], BF16, kind=kk).ap()
        dbg_h["dbg_w"] = nc.dram_tensor("dbg_w", [T, NEXP], BF16, kind=kk).ap()
        for nm in ("t0_pb", "t0_qe", "t0_ke", "t0_ve", "t0_ab", "t0_qkv", "t0_attn", "t0_z1", "t0_xn"):
            dt = F32 if nm in ("t0_z1",) else BF16
            dbg_h[nm] = nc.dram_tensor(nm, [128, 2048], dt, kind=kk).ap()

    ident_bf = nc.alloc_sbuf_tensor("ident_bf", [128, 128], BF16)
    ident_f = nc.alloc_sbuf_tensor("ident_f", [128, 128], F32)
    ones_bf = nc.alloc_sbuf_tensor("ones_bf", [1, 128], BF16)
    iota_t = nc.alloc_sbuf_tensor("iota_t", [128, 128], F32)
    qb_r = nc.alloc_sbuf_tensor("qb_r", [1, D], BF16)
    kb_r = nc.alloc_sbuf_tensor("kb_r", [1, D], BF16)
    vb_r = nc.alloc_sbuf_tensor("vb_r", [1, D], BF16)
    ob_r = nc.alloc_sbuf_tensor("ob_r", [1, D], BF16)
    projb_r = nc.alloc_sbuf_tensor("projb_r", [1, D], BF16)
    bd_r = nc.alloc_sbuf_tensor("bd_r", [1, D], BF16)
    fcb_c = nc.alloc_sbuf_tensor("fcb_c", [128, 32], F32)
    bg_c = nc.alloc_sbuf_tensor("bg_c", [128, 32], F32)
    bu_c = nc.alloc_sbuf_tensor("bu_c", [128, 32], F32)
    ln1w_c = nc.alloc_sbuf_tensor("ln1w_c", [128, 8], F32)
    ln1b_c = nc.alloc_sbuf_tensor("ln1b_c", [128, 8], F32)
    ln2w_c = nc.alloc_sbuf_tensor("ln2w_c", [128, 8], F32)
    ln2b_c = nc.alloc_sbuf_tensor("ln2b_c", [128, 8], F32)
    rows32 = nc.alloc_sbuf_tensor("rows32", [32, 128], F32)
    keysb = nc.alloc_sbuf_tensor("keysb", [128, 8 * NK], BF16)
    stat6 = nc.alloc_sbuf_tensor("stat6", [128, 12], F32)
    mv2 = nc.alloc_sbuf_tensor("mv2", [128, 2], F32)
    lnsc = nc.alloc_sbuf_tensor("lnsc", [128, 16], F32)
    sc_a = nc.alloc_sbuf_tensor("sc_a", [128, 4], F32)
    s8b = nc.alloc_sbuf_tensor("s8b", [128, 64], F32)
    cgrid = nc.alloc_sbuf_tensor("cgrid", [128, 256], F32)
    w8b = nc.alloc_sbuf_tensor("w8b", [128, 32], F32)
    w8s = nc.alloc_sbuf_tensor("w8s", [128, 32], F32)
    e8b = nc.alloc_sbuf_tensor("e8b", [128, 32], F32)
    zb = nc.alloc_sbuf_tensor("zb", [128, 4], F32)
    rzb = nc.alloc_sbuf_tensor("rzb", [128, 4], F32)
    emnb = nc.alloc_sbuf_tensor("emnb", [128, 4], F32)
    scaleb = nc.alloc_sbuf_tensor("scaleb", [128, 4], F32)
    thrb = nc.alloc_sbuf_tensor("thrb", [128, 4], F32)
    thrsb = nc.alloc_sbuf_tensor("thrsb", [128, 4], F32)
    axpre = nc.alloc_sbuf_tensor("axpre", [128, 512], F32)
    negc = nc.alloc_sbuf_tensor("negc", [128, 1], F32)
    epsb = nc.alloc_sbuf_tensor("epsb", [128, 1], F32)

    XO = (229376 - nc.sbuf_bytes_remaining + 31) & ~31
    assert XO + 65536 + 65536 + 32768 + 25600 <= 229376, f"SBUF overflow: XO={XO}"
    xbuf = nc.alloc_sbuf_tensor_at("xbuf", [128, NT * D], F32, offset=XO)
    WO = XO + 65536
    qT_sb = nc.alloc_sbuf_tensor_at("qT_sb", [128, 8 * D], BF16, offset=WO)
    kT_sb = nc.alloc_sbuf_tensor_at("kT_sb", [128, 8 * D], BF16, offset=WO + 16384)
    vT_sb = nc.alloc_sbuf_tensor_at("vT_sb", [128, 8 * D], BF16, offset=WO + 32768)
    oT_sb = nc.alloc_sbuf_tensor_at("oT_sb", [128, 8 * D], BF16, offset=WO + 49152)
    fcT_s = nc.alloc_sbuf_tensor_at("fcT_s", [128, 2 * 1024], BF16, offset=WO)
    projT_s = nc.alloc_sbuf_tensor_at("projT_s", [128, 8 * 1024], BF16, offset=WO + 4096)
    hFg = nc.alloc_sbuf_tensor_at("hFg", [128, 8 * T], BF16, offset=WO + 20480)
    downR = nc.alloc_sbuf_tensor_at("downR", [128, 8 * NEXP], BF16, offset=WO)
    wgu_s = nc.alloc_sbuf_tensor_at("wgu_s", [128, 2 * 2048], BF16, offset=WO)
    wd_s = nc.alloc_sbuf_tensor_at("wd_s", [128, 8 * 1024], BF16, offset=WO + 8192)
    prodg = nc.alloc_sbuf_tensor_at("prodg", [128, 8 * T], BF16, offset=WO + 24576)
    upR = nc.alloc_sbuf_tensor_at("upR", [128, 32 * D], BF16, offset=WO)
    F2O = WO + 65536
    f2 = nc.alloc_sbuf_tensor_at("f2", [128, 8 * T], BF16, offset=F2O)
    AO = F2O + 32768
    xnF_t = nc.alloc_sbuf_tensor_at("xnF_t", [128, D], BF16, offset=AO)
    pbA_t = nc.alloc_sbuf_tensor_at("pbA_t", [128, D], BF16, offset=AO + 2048)
    qe_t = nc.alloc_sbuf_tensor_at("qe_t", [128, D], BF16, offset=AO + 4096)
    ke_t = nc.alloc_sbuf_tensor_at("ke_t", [128, D], BF16, offset=AO + 6144)
    ve_t = nc.alloc_sbuf_tensor_at("ve_t", [128, D], BF16, offset=AO + 8192)
    pb_t = nc.alloc_sbuf_tensor_at("pb_t", [128, 2048], BF16, offset=AO + 10240)
    tmpa_t = nc.alloc_sbuf_tensor_at("tmpa_t", [128, D], F32, offset=AO + 10240)
    attnF_t = nc.alloc_sbuf_tensor_at("attnF_t", [128, D], BF16, offset=AO + 14336)
    ab_t = nc.alloc_sbuf_tensor_at("ab_t", [128, 256], BF16, offset=AO + 16384)
    qk_t = nc.alloc_sbuf_tensor_at("qk_t", [128, 16], F32, offset=AO + 16896)
    rq_t = nc.alloc_sbuf_tensor_at("rq_t", [128, 16], F32, offset=AO + 16960)
    qkv_t = nc.alloc_sbuf_tensor_at("qkv_t", [128, D], BF16, offset=AO + 17024)
    z1_t = nc.alloc_sbuf_tensor_at("z1_t", [128, D], F32, offset=AO + 19072)
    e1E = nc.alloc_sbuf_tensor_at("e1E", [128, NEXP], BF16, offset=AO)
    pgE = nc.alloc_sbuf_tensor_at("pgE", [128, 2048], F32, offset=AO + 8192)
    mskE = nc.alloc_sbuf_tensor_at("mskE", [128, 2048], BF16, offset=AO + 16384)
    wch = nc.alloc_sbuf_tensor_at("wch", [128, 512], BF16, offset=AO + 20480)
    silch = nc.alloc_sbuf_tensor_at("silch", [128, 512], BF16, offset=AO + 21504)
    simsb = nc.alloc_sbuf_tensor_at("simsb", [128, 512], F32, offset=AO + 22528)
    mskb = nc.alloc_sbuf_tensor_at("mskb", [128, 512], mybir.dt.uint8, offset=AO + 24576)
    qq_dram = nc.dram_tensor("qqspill", [4 * 128, T], BF16, kind="Internal").ap()
    wtin = nc.alloc_sbuf_tensor_at("wtin", [128, 2 * NEXP], BF16, offset=AO)
    wtt = nc.alloc_sbuf_tensor_at("wtt", [128, 32 * 128], BF16, offset=AO + 16384)
    wch2 = nc.alloc_sbuf_tensor_at("wch2", [128, 512], BF16, offset=AO + 27136)
    sgch = nc.alloc_sbuf_tensor_at("sgch", [128, 512], BF16, offset=AO + 25088)
    qqst = nc.alloc_sbuf_tensor_at("qqst", [64, 8 * 128], BF16, offset=AO + 25088)

    PA = nc.alloc_psum_tensor("PA", [128, 1024], F32)
    PB = nc.alloc_psum_tensor("PB", [128, 1024], F32)
    PC = nc.alloc_psum_tensor("PC", [128, 1024], F32)
    PT = nc.alloc_psum_tensor("PT", [128, 1024], BF16)
    PS = nc.alloc_psum_tensor("PS", [128, 512], F32)

    B = {}

    def buf(nm):
        if nm not in B:
            B[nm] = Buf(nm)
        return B[nm]

    P = Prog()

    REGION = {}
    for nm in ("qT", "kT", "vT", "oT"):
        REGION[nm] = ("WREG", "att")
    for nm in ("fcw0", "fcw1", "hFg") + tuple(f"pjw{j}" for j in range(8)):
        REGION[nm] = ("WREG", "fc")
    REGION["down"] = ("WREG", "e1w")
    for nm in ("wgw0", "wgw1", "wuw0", "wuw1", "prodg") + tuple(f"wdw{j}" for j in range(8)):
        REGION[nm] = ("WREG", "mlp")
    REGION["upw"] = ("WREG", "e2w")
    for nm in ("xn_t", "pbA", "qe", "ke", "ve", "pb", "attnF", "A", "qk", "rq", "qkv", "z1"):
        REGION[nm] = ("RA", "attn")
    for nm in ("E", "Pg0", "Pg1", "Emask0", "Emask1", "sims", "msk", "wc0", "sil0",
               "qqt", "qqsb", "wqs"):
        REGION[nm] = ("RA", "e1a")
    for nm in ("wt0", "wt1", "wtt"):
        REGION[nm] = ("RA", "e2a")
    REGION["sg"] = ("RA", "mlpa")
    _rstate = {}

    def op(engine, emit, r=(), w=(), dma=False, kind=""):
        extra = []
        for nm in tuple(r) + tuple(w):
            reg = REGION.get(nm)
            if reg is None:
                continue
            region, group = reg
            st = _rstate.setdefault(region, {"group": group, "cur": [], "prev": []})
            if st["group"] != group:
                st["prev"] = st["cur"]
                st["cur"] = []
                st["group"] = group
            extra.extend(st["prev"])
        oid = P.add(engine, emit, [buf(n) for n in r], [buf(n) for n in w], dma, kind)
        eng_, emit_, dset_, dma_, kind_ = P.ops[oid]
        dset_ = set(dset_) | set(extra)
        dset_.discard(oid)
        P.ops[oid] = (eng_, emit_, dset_, dma_, kind_)
        for nm in tuple(r) + tuple(w):
            reg = REGION.get(nm)
            if reg is not None:
                _rstate[reg[0]]["cur"].append(oid)
        return oid

    def fence():
        pass

    def xt(i):
        return xbuf[:, i * D:(i + 1) * D]

    op("pool", lambda g: g.iota(iota_t[:], pattern=[[1, 128]], base=0, channel_multiplier=-1,
                                allow_small_or_imprecise_dtypes=True), w=["iota"])
    op("pool", lambda g: g.memset(ones_bf[:], 1.0), w=["ones"])
    op("pool", lambda g: g.memset(negc[:], -1e9), w=["negc"])
    op("pool", lambda g: g.memset(epsb[:], EPS), w=["epsb"])
    op("dve", lambda v: v.tensor_scalar(ident_bf[:], iota_t[:], 0.0, None, ALU.is_equal),
       r=["iota"], w=["identbf"])
    op("dve", lambda v: v.tensor_scalar(ident_f[:], iota_t[:], 0.0, None, ALU.is_equal),
       r=["iota"], w=["identf"])
    op("sp", lambda s: s.dma_start(out=xbuf.ap().rearrange("p (n d) -> p n d", d=D),
                                   in_=x_h.rearrange("(n p) d -> p n d", p=128)),
       w=["x"], dma=True)
    for nm, hsrc, dst in (("qb", qb_h, qb_r), ("kb", kb_h, kb_r), ("vb", vb_h, vb_r),
                          ("ob", ob_h, ob_r), ("projb", projb_h, projb_r), ("bd", bd_h, bd_r)):
        op("pool", lambda g, hs=hsrc, dd=dst: g.dma_start(out=dd[:], in_=hs[:, :]),
           w=[nm], dma=True)

    def col_build(name, hsrc, nvec, dst):
        op("sp", lambda s: s.dma_start(out=rows32[0:nvec, :],
                                       in_=hsrc.rearrange("(c p) -> c p", p=128)),
           w=["rows32"], dma=True)
        op("pe", lambda t: t.transpose(PS[:, 0:nvec], rows32[0:nvec, :], ident_f[0:nvec, 0:nvec]),
           r=["rows32", "identf"], w=["PS"])
        op("dve", lambda v: v.tensor_copy(dst[:, 0:nvec], PS[:, 0:nvec]),
           r=["PS"], w=[name])

    col_build("fcbc", fcb_h, 32, fcb_c)
    col_build("bgc", bg_h, 32, bg_c)
    col_build("buc", bu_h, 32, bu_c)
    col_build("ln1wc", ln1w_h, 8, ln1w_c)
    col_build("ln1bc", ln1b_h, 8, ln1b_c)
    col_build("ln2wc", ln2w_h, 8, ln2w_c)
    col_build("ln2bc", ln2b_h, 8, ln2b_c)

    op("pool", lambda g: g.dma_start(
        out=keysb[0:64, :].rearrange("n (h p k) -> n h p k", h=4, p=2),
        in_=keysT_h.transpose([2, 0, 1, 3])),
       w=["keys"], dma=True)
    op("pool", lambda g: g.dma_start(
        out=keysb[64:128, :].rearrange("n (h p k) -> n h p k", h=4, p=2),
        in_=keysT_h.transpose([2, 0, 1, 3])),
       w=["keys"], dma=True)
    for nm, hsrc, dst in (("qT", qwT_h, qT_sb), ("kT", kwT_h, kT_sb),
                          ("vT", vwT_h, vT_sb), ("oT", owT_h, oT_sb)):
        op("pool", lambda g, hs=hsrc, dd=dst: g.dma_start(
            out=dd.ap().rearrange("p (c n) -> p c n", n=D),
            in_=hs.rearrange("(c p) n -> p c n", p=128)),
            w=[nm], dma=True)

    def ln_stats(xap, srcbufs, mu_col, r_col, s_col=None):
        op("dve", lambda v: v.bn_stats(stat6[:, 0:6], xap[:, 0:512]), r=srcbufs, w=["stat6"], kind="bn")
        op("dve", lambda v: v.bn_stats(stat6[:, 6:12], xap[:, 512:1024]), r=srcbufs, w=["stat6"], kind="bn")
        op("dve", lambda v: v.bn_aggr(mv2[:], stat6[:]), r=["stat6"], w=["mv2"], kind="bn")
        op("dve", lambda v: v.tensor_copy(lnsc[:, mu_col:mu_col + 1], mv2[:, 0:1]), r=["mv2"], w=["lnsc"])
        op("act", lambda a: a.activation(sc_a[:, 0:1], mv2[:, 1:2], AF.Ln, bias=epsb[:, 0:1]),
           r=["mv2", "epsb"], w=["sca"])
        op("act", lambda a: a.activation(lnsc[:, r_col:r_col + 1], sc_a[:, 0:1], AF.Exp, scale=-0.5),
           r=["sca"], w=["lnsc"])
        if s_col is not None:
            op("dve", lambda v: v.tensor_tensor(sc_a[:, 2:3], lnsc[:, r_col:r_col + 1],
                                                lnsc[:, r_col:r_col + 1], ALU.mult),
               r=["lnsc"], w=["sca"])
            op("dve", lambda v: v.tensor_tensor(sc_a[:, 2:3], sc_a[:, 2:3], mv2[:, 1:2], ALU.mult),
               r=["sca", "mv2"], w=["sca"])
            op("act", lambda a: a.activation(sc_a[:, 3:4], sc_a[:, 2:3], AF.Ln, bias=epsb[:, 0:1]),
               r=["sca", "epsb"], w=["sca"])
            op("act", lambda a: a.activation(sc_a[:, 2:3], sc_a[:, 3:4], AF.Exp, scale=-0.5),
               r=["sca"], w=["sca"])
            op("dve", lambda v: v.tensor_tensor(lnsc[:, s_col:s_col + 1], lnsc[:, r_col:r_col + 1],
                                                sc_a[:, 2:3], ALU.mult), r=["sca", "lnsc"], w=["lnsc"])

    fence()

    def attn_s1(i):
        p = i % 2
        ln_stats(xt(i), ["x"], 0 + p, 2 + p, 4 + p)
        op("dve", lambda v, ii=i, pp=p: v.tensor_scalar(
            xnF_t[:], xt(ii), lnsc[:, 0 + pp:1 + pp], lnsc[:, 4 + pp:5 + pp], ALU.subtract, ALU.mult),
            r=["x", "lnsc"], w=["xn_t"])

    def attn_s2(i):
        for c in range(8):
            op("pe", lambda t, cc=c: t.transpose(
                PT[:, cc * 128:(cc + 1) * 128], xnF_t[:, cc * 128:(cc + 1) * 128], ident_bf[:]),
                r=["xn_t", "identbf"], w=["PT"])
        for c in range(8):
            op("dve", lambda v, cc=c: v.tensor_scalar(
                pbA_t[:, cc * 128:(cc + 1) * 128], PT[:, cc * 128:(cc + 1) * 128],
                ln1w_c[:, cc:cc + 1], ln1b_c[:, cc:cc + 1], ALU.mult, ALU.add),
                r=["PT", "ln1wc", "ln1bc"], w=["pbA"])

        def proj_mm(wsb, wname, brow, bname, pdst, pname):
            for nb in range(2):
                op("pe", lambda t, n=nb: t.matmul(
                    pdst[:, n * 512:(n + 1) * 512], ones_bf[:, :], brow[:, n * 512:(n + 1) * 512],
                    start=True, stop=False), r=["ones", bname], w=[pname])
                for k in range(8):
                    op("pe", lambda t, n=nb, kk=k: t.matmul(
                        pdst[:, n * 512:(n + 1) * 512],
                        pbA_t[:, kk * 128:(kk + 1) * 128],
                        wsb[:, kk * D + n * 512: kk * D + (n + 1) * 512],
                        start=False, stop=(kk == 7)), r=["pbA", wname], w=[pname])

        proj_mm(qT_sb, "qT", qb_r, "qb", PA, "PA")
        proj_mm(kT_sb, "kT", kb_r, "kb", PB, "PB")
        proj_mm(vT_sb, "vT", vb_r, "vb", PC, "PC")

    def attn_s3(i):
        def fold_tree(src_ap, src_n, src_buf, dst_ap, dst_buf, scratch, src_ng):
            gh = src_ng // 2
            cur_ap, cur_buf, d = src_ap, src_buf, src_n
            s = 0
            while d > 1:
                h = d // 2
                if h == 1:
                    out_ap, out_buf = dst_ap, dst_buf
                else:
                    out_ap = scratch[s % len(scratch)][0](h)
                    out_buf = scratch[s % len(scratch)][1]
                for half in range(2):
                    gs = slice(half * gh, (half + 1) * gh)
                    op("dve", lambda v, o=out_ap, a=cur_ap, hh=h, g_=gs: v.tensor_tensor(
                        o[:, g_, :], a[:, g_, 0:hh], a[:, g_, hh:2 * hh], ALU.add),
                       r=[cur_buf], w=[out_buf], kind=FOLD_KIND)
                cur_ap, cur_buf, d = out_ap, out_buf, h
                s += 1

        for hh in range(8):
            in0 = qe_t.ap()[:, hh * 128:(hh + 1) * 128].rearrange("p (a d) -> p a d", a=2) \
                .unsqueeze(2).broadcast_to([128, 2, 16, 64])
            in1 = ke_t.ap().rearrange("p (b d) -> p b d", b=16).unsqueeze(1).broadcast_to([128, 2, 16, 64])
            outp = pb_t.ap().rearrange("p (a b d) -> p a b d", a=2, b=16)
            for ah in range(2):
                op("dve", lambda v, i0=in0, i1=in1, oo=outp, a_=ah: v.tensor_tensor(
                    oo[:, a_:a_ + 1], i0[:, a_:a_ + 1], i1[:, a_:a_ + 1], ALU.mult),
                   r=["qe", "ke"], w=["pb"])
            scr = [(lambda c: attnF_t.ap()[:, 0:32 * c].rearrange("p (g d) -> p g d", g=32), "attnF"),
                   (lambda c: pb_t.ap()[:, 0:32 * c].rearrange("p (g d) -> p g d", g=32), "pb")]
            fold_tree(pb_t.ap().rearrange("p (g d) -> p g d", d=64), 64, "pb",
                      ab_t.ap()[:, hh * 32:(hh + 1) * 32].unsqueeze(2), "A", scr, 32)
        scrq = [(lambda c: attnF_t.ap()[:, 0:16 * c].rearrange("p (g d) -> p g d", g=16), "attnF"),
                (lambda c: pb_t.ap()[:, 0:16 * c].rearrange("p (g d) -> p g d", g=16), "pb")]
        fold_tree(ab_t.ap().rearrange("p (g b) -> p g b", b=16), 16, "A",
                  qk_t.ap().unsqueeze(2), "qk", scrq, 16)
        op("dve", lambda v: v.reciprocal(rq_t[:], qk_t[:]), r=["qk"], w=["rq"], kind="recip")
        for hh in range(8):
            in0 = ab_t.ap()[:, hh * 32:(hh + 1) * 32].rearrange("p (a b) -> p a b", a=2) \
                .unsqueeze(3).broadcast_to([128, 2, 16, 64])
            in1 = ve_t.ap().rearrange("p (b e) -> p b e", b=16).unsqueeze(1).broadcast_to([128, 2, 16, 64])
            outp = pb_t.ap().rearrange("p (a e b) -> p a b e", a=2, e=64, b=16)
            for ah in range(2):
                op("dve", lambda v, i0=in0, i1=in1, oo=outp, a_=ah: v.tensor_tensor(
                    oo[:, a_:a_ + 1], i0[:, a_:a_ + 1], i1[:, a_:a_ + 1], ALU.mult),
                   r=["A", "ve"], w=["pb"])
            scr2 = [(lambda c: attnF_t.ap()[:, 0:128 * c].rearrange("p (g d) -> p g d", g=128), "attnF"),
                    (lambda c: pb_t.ap()[:, 0:128 * c].rearrange("p (g d) -> p g d", g=128), "pb")]
            fold_tree(pb_t.ap().rearrange("p (g b) -> p g b", b=16), 16, "pb",
                      qkv_t.ap()[:, hh * 128:(hh + 1) * 128].unsqueeze(2), "qkv", scr2, 128)
        op("dve", lambda v: v.tensor_tensor(
            qkv_t.ap().rearrange("p (h e) -> p h e", h=16),
            qkv_t.ap().rearrange("p (h e) -> p h e", h=16),
            rq_t.ap().unsqueeze(2).broadcast_to([128, 16, 64]), ALU.mult),
           r=["qkv", "rq"], w=["qkv"])

    def attn_s4(i):
        def elu_evac(psrc, pname, dst, dname):
            op("dve", lambda v: v.tensor_scalar(tmpa_t[:], psrc[:], 0.0, None, ALU.min),
               r=[pname], w=["pb"])
            op("act", lambda a: a.activation(z1_t[:], tmpa_t[:], AF.Exp), r=["pb"], w=["z1"])
            op("dve", lambda v: v.scalar_tensor_tensor(dst[:], psrc[:], 0.0, z1_t[:], ALU.max, ALU.add),
               r=[pname, "z1"], w=[dname])

        elu_evac(PA, "PA", qe_t, "qe")
        elu_evac(PB, "PB", ke_t, "ke")
        op("dve", lambda v: v.tensor_copy(ve_t[:], PC[:]), r=["PC"], w=["ve"])

    def attn_s5(i):
        for c in range(8):
            op("pe", lambda t, cc=c: t.transpose(
                PT[:, cc * 128:(cc + 1) * 128], qkv_t[:, cc * 128:(cc + 1) * 128], ident_bf[:]),
                r=["qkv", "identbf"], w=["PT"])
        for c in range(8):
            op("act", lambda a, cc=c: a.activation(
                attnF_t[:, cc * 128:(cc + 1) * 128], PT[:, cc * 128:(cc + 1) * 128], AF.Copy),
                r=["PT"], w=["attnF"])
        for nb in range(2):
            op("pe", lambda t, n=nb: t.matmul(
                PA[:, n * 512:(n + 1) * 512], ones_bf[:, :], ob_r[:, n * 512:(n + 1) * 512],
                start=True, stop=False), r=["ones", "ob"], w=["PA"])
            for k in range(8):
                op("pe", lambda t, n=nb, kk=k: t.matmul(
                    PA[:, n * 512:(n + 1) * 512],
                    attnF_t[:, kk * 128:(kk + 1) * 128],
                    oT_sb[:, kk * D + n * 512: kk * D + (n + 1) * 512],
                    start=False, stop=(kk == 7)), r=["attnF", "oT"], w=["PA"])

    def attn_s6(i):
        p = i % 2
        op("dve", lambda v, ii=i, pp=p: v.tensor_scalar(
            z1_t[:], xt(ii), lnsc[:, 0 + pp:1 + pp], lnsc[:, 2 + pp:3 + pp], ALU.subtract, ALU.mult),
            r=["x", "lnsc"], w=["z1"])
        op("dve", lambda v: v.tensor_tensor(z1_t[:], z1_t[:], PA[:], ALU.add),
           r=["z1", "PA"], w=["z1"])
        op("dve", lambda v, ii=i: v.tensor_tensor(xt(ii), xt(ii), z1_t[:], ALU.add),
           r=["z1", "x"], w=["x"])
        ln_stats(z1_t, ["z1"], 6 + p, 8 + p)
        op("dve", lambda v, pp=p: v.tensor_scalar(
            attnF_t[:], z1_t[:], lnsc[:, 6 + pp:7 + pp], lnsc[:, 8 + pp:9 + pp], ALU.subtract, ALU.mult),
            r=["z1", "lnsc"], w=["attnF"])
        for c in range(8):
            op("pe", lambda t, cc=c: t.transpose(
                PT[:, cc * 128:(cc + 1) * 128], attnF_t[:, cc * 128:(cc + 1) * 128], ident_bf[:]),
                r=["attnF", "identbf"], w=["PT"])
        for c in range(8):
            op("dve", lambda v, cc=c, ii=i: v.tensor_scalar(
                f2[:, cc * T + ii * 128: cc * T + (ii + 1) * 128],
                PT[:, cc * 128:(cc + 1) * 128],
                ln2w_c[:, cc:cc + 1], ln2b_c[:, cc:cc + 1], ALU.mult, ALU.add),
                r=["PT", "ln2wc", "ln2bc"], w=["f2"])

    for i in range(NT + 1):
        if i < NT:
            attn_s1(i)
            attn_s2(i)
        if i > 0:
            attn_s3(i - 1)
        if i < NT:
            attn_s4(i)
        if i > 0:
            attn_s5(i - 1)
            attn_s6(i - 1)

    fence()
    for grp in range(4):
        for mi in range(8):
            m = grp * 8 + mi
            sl = (m % 2) * 1024
            op("pool", lambda g, mm=m, ss=sl: g.dma_start(
                out=fcT_s[:, ss:ss + 1024].rearrange("p (k m) -> p k m", m=128),
                in_=fcT_h[:, mm * 128:(mm + 1) * 128].rearrange("(k p) m -> p k m", p=128)),
                w=[f"fcw{m % 2}"], dma=True)
            for nb in range(4):
                pdst = PS[:] if nb % 2 == 0 else PA[:, 0:512]
                pnm = "PS" if nb % 2 == 0 else "PA"
                for k in range(8):
                    op("pe", lambda t, ss=sl, n=nb, kk=k, mm=m, pd=pdst, pn=pnm: t.matmul(
                        pd,
                        fcT_s[:, ss + kk * 128: ss + (kk + 1) * 128],
                        f2[:, kk * T + n * 512: kk * T + (n + 1) * 512],
                        start=(kk == 0), stop=(kk == 7)),
                        r=[f"fcw{m % 2}", "f2"], w=[pnm])
                op("act", lambda a, mm=m, ii=mi, n=nb, pd=pdst, pn=pnm: a.activation(
                    hFg[:, ii * T + n * 512: ii * T + (n + 1) * 512], pd, AF.Gelu,
                    bias=fcb_c[:, mm:mm + 1]),
                    r=[pnm, "fcbc"], w=["hFg"])
            op("pool", lambda g, mm=m, ii=mi: g.dma_start(
                out=projT_s[:, ii * 1024:(ii + 1) * 1024],
                in_=projT_h[mm * 128:(mm + 1) * 128, :]),
                w=[f"pjw{mi}"], dma=True)
        if dbg and grp == 0:
            op("sp", lambda s: s.dma_start(out=dbg_h["dbg_hf"][:, :], in_=hFg[:]), r=["hFg"], w=["dbg"], dma=True)
        for ts in range(NT):
            for mi in range(8):
                m = grp * 8 + mi
                for nb in range(2):
                    op("pe", lambda t, ii=mi, n=nb, tt=ts, g2=grp: t.matmul(
                        PB[:, n * 512:(n + 1) * 512],
                        hFg[:, ii * T + tt * 128: ii * T + (tt + 1) * 128],
                        projT_s[:, ii * 1024 + n * 512: ii * 1024 + (n + 1) * 512],
                        start=(ii == 0), stop=(ii == 7 and g2 != 0)),
                        r=["hFg", f"pjw{mi}"], w=["PB"])
            if grp == 0:
                for nb in range(2):
                    op("pe", lambda t, n=nb: t.matmul(
                        PB[:, n * 512:(n + 1) * 512], ones_bf[:, :],
                        projb_r[:, n * 512:(n + 1) * 512], start=False, stop=True),
                        r=["ones", "projb"], w=["PB"])
            op("dve", lambda v, tt=ts: v.tensor_tensor(xt(tt), xt(tt), PB[:], ALU.add),
               r=["PB", "x"], w=["x"])

    if dbg:
        for i in range(NT):
            op("sp", lambda s, ii=i: s.dma_start(out=dbg_h["dbg_fc"][ii * 128:(ii + 1) * 128, :], in_=xt(ii)),
               r=["x"], w=["dbg"], dma=True)
    fence()
    for i in range(NT):
        ln_stats(xt(i), ["x"], 10, 11)
        op("dve", lambda v, ii=i: v.tensor_scalar(
            attnF_t[:], xt(ii), lnsc[:, 10:11], lnsc[:, 11:12], ALU.subtract, ALU.mult),
            r=["x", "lnsc"], w=["attnF"])
        for c in range(8):
            op("pe", lambda t, cc=c: t.transpose(
                PT[:, cc * 128:(cc + 1) * 128], attnF_t[:, cc * 128:(cc + 1) * 128], ident_bf[:]),
                r=["attnF", "identbf"], w=["PT"])
        for c in range(8):
            op("act", lambda a, cc=c, ii=i: a.activation(
                f2[:, cc * T + ii * 128: cc * T + (ii + 1) * 128],
                PT[:, cc * 128:(cc + 1) * 128], AF.Copy),
                r=["PT"], w=["f2"])

    if dbg:
        op("sp", lambda s: s.dma_start(out=dbg_h["dbg_f2"][:, :], in_=f2[:]), r=["f2"], w=["dbg"], dma=True)
    for mc in range(4):
        op("pool", lambda g, m=mc: g.dma_start(
            out=e1E[:, 0:1024].rearrange("p (k m) -> p k m", m=128),
            in_=wq_h[:, m * 128:(m + 1) * 128].rearrange("(k p) m -> p k m", p=128)),
            w=["wqs"], dma=True)
        for nb in range(4):
            pdst = PS[:] if nb % 2 == 0 else PA[:, 0:512]
            pnm = "PS" if nb % 2 == 0 else "PA"
            for k in range(8):
                op("pe", lambda t, n=nb, kk=k, pd=pdst, pn=pnm: t.matmul(
                    pd,
                    e1E[:, kk * 128:(kk + 1) * 128],
                    f2[:, kk * T + n * 512: kk * T + (n + 1) * 512],
                    start=(kk == 0), stop=(kk == 7)),
                    r=["wqs", "f2"], w=[pnm])
            op("dve", lambda v, n=nb, pd=pdst, pn=pnm: v.tensor_copy(mskE[:, n * 512:(n + 1) * 512], pd),
               r=[pnm], w=["qqsb"])
        op("sp", lambda s, m=mc: s.dma_start(
            out=qq_dram[m * 128:(m + 1) * 128, :], in_=mskE[:]),
            r=["qqsb"], w=["qqd"], dma=True)

    fence()
    op("pool", lambda g: g.dma_start(
        out=downR.ap().rearrange("p (c e) -> p c e", e=NEXP),
        in_=downT_h.rearrange("(c p) e -> p c e", p=128)),
        w=["down"], dma=True)
    EBSLOT = [
        (PA[:, 0:512], "PAl", "PA"), (PA[:, 512:1024], "PAh", "PA"),
        (PB[:, 0:512], "PBl", "PB"), (PB[:, 512:1024], "PBh", "PB"),
        (PC[:, 0:512], "PCl", "PC"), (PC[:, 512:1024], "PCh", "PC"),
        (PS[:, 0:512], "PS", "PS"), (PA[:, 0:512], "PAl", "PA"),
    ]
    WCH = (wch, wch2)
    for i in range(NT):
        if i == 0:
            op("sp", lambda s: s.dma_start(
                out=qqst[:, :].rearrange("p (c t) -> p c t", t=128),
                in_=qq_dram[:, 0:128].rearrange("(c p) t -> p c t", p=64)),
                r=["qqd"], w=["qqt"], dma=True)
        for ph in range(8):
            p_, h_ = ph // 4, ph % 4
            kcol = (h_ * 2 + p_) * 64
            op("pe", lambda t, pp=ph, kc=kcol: t.matmul(
                PS[:, pp * 64:(pp + 1) * 64],
                qqst[:, pp * 128:(pp + 1) * 128],
                keysb[0:64, kc:kc + 64],
                start=True, stop=True),
                r=["qqt", "keys"], w=["PS"])
        op("dve", lambda v: v.tensor_copy(simsb[:], PS[:]), r=["PS"], w=["sims"])
        if i + 1 < NT:
            op("sp", lambda s, ii=i + 1: s.dma_start(
                out=qqst[:, :].rearrange("p (c t) -> p c t", t=128),
                in_=qq_dram[:, ii * 128:(ii + 1) * 128].rearrange("(c p) t -> p c t", p=64)),
                r=["qqd"], w=["qqt"], dma=True)
        def eb_matmuls(eb):
            pd, fn, cn = EBSLOT[eb]
            wnames = [fn, cn] if (i == 0 and fn != cn) else [fn]
            for k in range(8):
                op("pe", lambda t, n=eb, kk=k, ii=i, pp=pd: t.matmul(
                    pp,
                    f2[:, kk * T + ii * 128: kk * T + (ii + 1) * 128],
                    downR[:, kk * NEXP + n * 512: kk * NEXP + (n + 1) * 512],
                    start=(kk == 0), stop=(kk == 7)),
                    r=["f2", "down"], w=(wnames if k == 0 else [fn]))

        for eb in range(7):
            eb_matmuls(eb)
        for ph in range(8):
            op("dve", lambda v, pp=ph: v.max(s8b[:, pp * 8:(pp + 1) * 8], simsb[:, pp * 64:(pp + 1) * 64]),
               r=["sims"], w=["s8"], kind="max8")
        for ph in range(8):
            op("dve", lambda v, pp=ph: v.tensor_scalar(
                mskb[:, pp * 64:(pp + 1) * 64], simsb[:, pp * 64:(pp + 1) * 64],
                s8b[:, pp * 8 + 7: pp * 8 + 8], None, ALU.is_ge),
               r=["sims", "s8"], w=["msk"])

        op("dve", lambda v: v.tensor_tensor(
            cgrid.ap().rearrange("p (h a b) -> p h a b", h=4, a=8),
            s8b.ap()[:, 0:32].rearrange("p (h a) -> p h a", h=4).unsqueeze(3).broadcast_to([128, 4, 8, 8]),
            s8b.ap()[:, 32:64].rearrange("p (h b) -> p h b", h=4).unsqueeze(2).broadcast_to([128, 4, 8, 8]),
            ALU.add), r=["s8"], w=["C"])
        for h in range(4):
            op("dve", lambda v, hh=h: v.max(w8b[:, hh * 8:(hh + 1) * 8], cgrid[:, hh * 64:(hh + 1) * 64]),
               r=["C"], w=["w8"], kind="max8")
        op("dve", lambda v: v.tensor_tensor(
            w8s.ap().rearrange("p (h a) -> p h a", h=4),
            w8b.ap().rearrange("p (h a) -> p h a", h=4),
            w8b.ap().rearrange("p (h a) -> p h a", h=4)[:, :, 0:1].broadcast_to([128, 4, 8]),
            ALU.subtract), r=["w8"], w=["w8sb"])
        op("act", lambda a: a.activation(e8b[:], w8s[:], AF.Exp), r=["w8sb"], w=["e8"])
        op("dve", lambda v: v.reduce_sum(zb[:], e8b.ap().rearrange("p (h a) -> p h a", h=4), axis=AX.X),
           r=["e8"], w=["Z"], kind="reduce")
        op("act", lambda a: a.activation(rzb[:], zb[:], AF.Ln), r=["Z"], w=["rz"])
        op("dve", lambda v: v.tensor_tensor(
            scaleb[:], rzb[:],
            w8b.ap().rearrange("p (h a) -> p h a", h=4)[:, :, 0:1].rearrange("p h o -> p (h o)"),
            ALU.add), r=["rz", "w8"], w=["SCALE"])
        op("dve", lambda v: v.tensor_scalar(emnb[:], scaleb[:], -1.0, None, ALU.mult), r=["SCALE"], w=["emn"])

        op("dve", lambda v: v.select(axpre[:], mskb[:], simsb[:],
                                     negc.ap().broadcast_to([128, 512])),
           r=["msk", "sims", "negc"], w=["ef"], kind="sel")
        def grid_A(j):
            hf = j // 8
            h = (j % 8) // 2
            sub = j % 2
            par = j % 2
            pgn, mkn = f"Pg{par}", f"Emask{par}"
            pg = pgE.ap()[:, par * 1024:(par + 1) * 1024]
            mk = mskE.ap()[:, par * 1024:(par + 1) * 1024]
            in0 = axpre.ap()[:, h * 64 + hf * 32 + sub * 16: h * 64 + hf * 32 + (sub + 1) * 16] \
                .unsqueeze(2).broadcast_to([128, 16, 64])
            in1 = axpre.ap()[:, 256 + h * 64: 256 + (h + 1) * 64].unsqueeze(1).broadcast_to([128, 16, 64])
            op("dve", lambda e, i0=in0, i1=in1, pg_=pg: e.tensor_tensor(
                pg_.rearrange("p (a d) -> p a d", d=64), i0, i1, ALU.add),
               r=["ef"], w=[pgn])
            op("dve", lambda e, hh=h, pg_=pg, mk_=mk: e.tensor_scalar(
                mk_, pg_, w8b[:, hh * 8 + 7: hh * 8 + 8], None, ALU.is_ge),
               r=[pgn, "w8"], w=[mkn])
            eq = pgE.ap().bitcast(BF16)[:, par * 2048: par * 2048 + 1024]
            op("act", lambda a, hh=h, pg_=pg, eq_=eq: a.activation(
                eq_, pg_, AF.Exp, bias=emnb[:, hh:hh + 1]),
               r=[pgn, "emn"], w=[pgn])

        def grid_B(j):
            hf = j // 8
            h = (j % 8) // 2
            sub = j % 2
            par = j % 2
            pgn, mkn = f"Pg{par}", f"Emask{par}"
            eq = pgE.ap().bitcast(BF16)[:, par * 2048: par * 2048 + 1024]
            mk = mskE.ap()[:, par * 1024:(par + 1) * 1024]
            eslice = e1E.ap()[:, hf * 2048 + sub * 1024: hf * 2048 + (sub + 1) * 1024]
            op("dve", lambda e, eq_=eq, mk_=mk: e.tensor_tensor(mk_, eq_, mk_, ALU.mult),
               r=[mkn, pgn], w=[mkn])
            if h == 0:
                op("dve", lambda e, es=eslice, mk_=mk: e.tensor_copy(es, mk_),
                   r=[mkn], w=["E"])
            else:
                op("dve", lambda e, es=eslice, mk_=mk: e.tensor_tensor(es, es, mk_, ALU.add),
                   r=[mkn, "E"], w=["E"])

        for j in range(16):
            grid_A(j)
            if j >= 1:
                grid_B(j - 1)
        grid_B(15)
        def eb_evac(eb):
            pd, fn, cn = EBSLOT[eb]
            op("act", lambda a, pp=pd: a.activation(silch[:], pp, AF.Silu),
               r=([fn] if fn == cn else [fn, cn]), w=["sil0"])
            wc = WCH[eb % 2]
            wcn = f"wc{eb % 2}"
            op("dve", lambda v, n=eb, wc_=wc: v.tensor_tensor(
                wc_[:], e1E[:, n * 512:(n + 1) * 512], silch[:], ALU.mult),
               r=["E", "sil0"], w=[wcn])
            op("sp" if eb % 2 == 0 else "pool", lambda s, n=eb, ii=i, wc_=wc: s.dma_start(
                out=wspill_h[ii * 128:(ii + 1) * 128, n * 512:(n + 1) * 512],
                in_=wc_[:]),
                r=[wcn], w=["wsp"], dma=True)

        eb_evac(0)
        eb_matmuls(7)
        for eb in range(1, 8):
            eb_evac(eb)

    fence()
    for grp in range(4):
        for mi in range(8):
            m = grp * 8 + mi
            sl = (m % 2) * 2048
            op("pool", lambda g, mm=m, ss=sl: g.dma_start(
                out=wgu_s[:, ss:ss + 1024].rearrange("p (k m) -> p k m", m=128),
                in_=wg_h[:, mm * 128:(mm + 1) * 128].rearrange("(k p) m -> p k m", p=128)),
                w=[f"wgw{m % 2}"], dma=True)
            op("pool", lambda g, mm=m, ss=sl: g.dma_start(
                out=wgu_s[:, ss + 1024:ss + 2048].rearrange("p (k m) -> p k m", m=128),
                in_=wu_h[:, mm * 128:(mm + 1) * 128].rearrange("(k p) m -> p k m", p=128)),
                w=[f"wuw{m % 2}"], dma=True)
            for nb in range(4):
                gdst = PS[:] if nb % 2 == 0 else PC[:, 0:512]
                gnm = "PS" if nb % 2 == 0 else "PC"
                udst = PA[:, 0:512] if nb % 2 == 0 else PC[:, 512:1024]
                unm = "PA" if nb % 2 == 0 else "PC2"
                for k in range(8):
                    op("pe", lambda t, ss=sl, n=nb, kk=k, mm=m, pd=gdst, pn=gnm: t.matmul(
                        pd,
                        wgu_s[:, ss + kk * 128: ss + (kk + 1) * 128],
                        f2[:, kk * T + n * 512: kk * T + (n + 1) * 512],
                        start=(kk == 0), stop=(kk == 7)),
                        r=[f"wgw{m % 2}", "f2"], w=[gnm])
                for k in range(8):
                    op("pe", lambda t, ss=sl, n=nb, kk=k, mm=m, pd=udst, pn=unm: t.matmul(
                        pd,
                        wgu_s[:, ss + 1024 + kk * 128: ss + 1024 + (kk + 1) * 128],
                        f2[:, kk * T + n * 512: kk * T + (n + 1) * 512],
                        start=(kk == 0), stop=(kk == 7)),
                        r=[f"wuw{m % 2}", "f2"],
                        w=(["PC2", "PC"] if (m == 0 and nb == 1 and k == 0) else [unm]))
                sgd = sgch[:, 0:512]
                op("act", lambda a, mm=m, pd=gdst, pn=gnm: a.activation(
                    sgch[:], pd, AF.Silu, bias=bg_c[:, mm:mm + 1]),
                    r=[gnm, "bgc"], w=["sg"])
                op("dve", lambda v, mm=m, pd=udst, pn=unm: v.tensor_scalar(
                    pd, pd, bu_c[:, mm:mm + 1], None, ALU.add),
                    r=[unm, "buc"], w=[unm])
                op("dve", lambda v, ii=mi, n=nb, pd=udst, pn=unm: v.tensor_tensor(
                    prodg[:, ii * T + n * 512: ii * T + (n + 1) * 512], sgch[:], pd, ALU.mult),
                   r=["sg", unm], w=["prodg"])
            op("pool", lambda g, mm=m, ii=mi: g.dma_start(
                out=wd_s[:, ii * 1024:(ii + 1) * 1024],
                in_=wd_h[mm * 128:(mm + 1) * 128, :]),
                w=[f"wdw{mi}"], dma=True)
        for ts in range(NT):
            for mi in range(8):
                m = grp * 8 + mi
                for nb in range(2):
                    op("pe", lambda t, ii=mi, n=nb, tt=ts, g2=grp: t.matmul(
                        PB[:, n * 512:(n + 1) * 512],
                        prodg[:, ii * T + tt * 128: ii * T + (tt + 1) * 128],
                        wd_s[:, ii * 1024 + n * 512: ii * 1024 + (n + 1) * 512],
                        start=(ii == 0), stop=(ii == 7 and g2 != 0)),
                        r=["prodg", f"wdw{mi}"], w=["PB"])
            if grp == 0:
                for nb in range(2):
                    op("pe", lambda t, n=nb: t.matmul(
                        PB[:, n * 512:(n + 1) * 512], ones_bf[:, :],
                        bd_r[:, n * 512:(n + 1) * 512], start=False, stop=True),
                        r=["ones", "bd"], w=["PB"])
            op("dve", lambda v, tt=ts: v.tensor_tensor(xt(tt), xt(tt), PB[:], ALU.add),
               r=["PB", "x"], w=["x"])

    if dbg:
        for i in range(NT):
            op("sp", lambda s, ii=i: s.dma_start(out=dbg_h["dbg_mlp"][ii * 128:(ii + 1) * 128, :], in_=xt(ii)),
               r=["x"], w=["dbg"], dma=True)
    fence()
    op("pool", lambda g: g.dma_start(
        out=upR.ap().rearrange("p (c d) -> p c d", d=D),
        in_=up_h.rearrange("(c p) d -> p c d", p=128)),
        w=["upw"], dma=True)
    for i in range(NT):
        sl = (i % 2) * NEXP
        op("sp", lambda s, ii=i, ss=sl: s.dma_start(
            out=wtin[:, ss:ss + NEXP], in_=wspill_h[ii * 128:(ii + 1) * 128, :]),
            r=["wsp"], w=[f"wt{i % 2}"], dma=True)
        if dbg:
            op("sp", lambda s, ii=i, ss=sl: s.dma_start(
                out=dbg_h["dbg_w"][ii * 128:(ii + 1) * 128, :], in_=wtin[:, ss:ss + NEXP]),
                r=[f"wt{i % 2}"], w=["dbg"], dma=True)
        for r4 in range(4):
            for c in range(8):
                cc = r4 * 8 + c
                op("pe", lambda t, ch=cc, ss=sl, ii=i: t.transpose(
                    PT[:, (ch % 8) * 128:((ch % 8) + 1) * 128],
                    wtin[:, ss + ch * 128: ss + (ch + 1) * 128], ident_bf[:]),
                    r=[f"wt{i % 2}", "identbf"], w=["PT"])
            op("dve", lambda v, rr=r4: v.tensor_copy(
                wtt[:, rr * 1024:(rr + 1) * 1024], PT[:]), r=["PT"], w=["wtt"])
        updst = PB.ap() if i % 2 == 0 else PC.ap()
        upnm = "PB" if i % 2 == 0 else "PC"
        for nb in range(2):
            for c in range(32):
                op("pe", lambda t, ch=c, n=nb, pd=updst, pn=upnm: t.matmul(
                    pd[:, n * 512:(n + 1) * 512],
                    wtt[:, ch * 128:(ch + 1) * 128],
                    upR[:, ch * D + n * 512: ch * D + (n + 1) * 512],
                    start=(ch == 0), stop=(ch == 31)),
                    r=["wtt", "upw"], w=[upnm])
        op("dve", lambda v, ii=i, pd=updst, pn=upnm: v.tensor_tensor(xt(ii), xt(ii), pd[:], ALU.add),
           r=[upnm, "x"], w=["x"])

    fence()
    for i in range(NT):
        op("sp", lambda s, ii=i: s.dma_start(
            out=y_h[ii * 128:(ii + 1) * 128, :], in_=xt(ii)),
            r=["x"], w=["yout"], dma=True)
    op("pe", lambda t: t.nop(), r=["yout"])
    op("act", lambda a: a.nop(), r=["yout"])
    op("dve", lambda v: v.nop(), r=["yout"])
    op("pool", lambda g: g.nop(), r=["yout"])
    op("sp", lambda s: s.nop(), r=["yout"])

    if max_ops is not None:
        P.ops = P.ops[:max_ops]
        for i in range(len(P.ops)):
            eng, emit, deps, dma = P.ops[i]
            P.ops[i] = (eng, emit, {d for d in deps if d < max_ops}, dma)
    sems = {}
    for e in ENGINES:
        sems[(e, False)] = nc.alloc_semaphore("semc_" + e)
        sems[(e, True)] = nc.alloc_semaphore("semd_" + e)
    runner = P.emit(sems)
    with nc.allow_low_precision(reason="bf16 matmul-class intermediates; reductions accumulate fp32 internally"):
        with nc.Block() as block:
            block.tensor(lambda t: runner(t, "pe"))
            block.scalar(lambda a: runner(a, "act"))
            block.vector(lambda v: runner(v, "dve"))
            block.gpsimd(lambda g: runner(g, "pool"))
            block.sync(lambda s: runner(s, "sp"))
    return nc


_NC_CACHE = {}


def _get_nc():
    if "nc" not in _NC_CACHE:
        _NC_CACHE["nc"] = build_nc()
    return _NC_CACHE["nc"]


def kernel(**inputs):
    xfull = np.ascontiguousarray(np.asarray(inputs["L0_attn_x"]).reshape(-1, D))
    shards = np.split(xfull, NCORES, axis=0)

    def c(a):
        return np.ascontiguousarray(np.asarray(a))

    w = {
        "qwT": c(inputs["L0_attn_q_w"]).T.copy(), "kwT": c(inputs["L0_attn_k_w"]).T.copy(),
        "vwT": c(inputs["L0_attn_v_w"]).T.copy(), "owT": c(inputs["L0_attn_o_w"]).T.copy(),
        "fcT": c(inputs["L0_attn_fc_w"]).T.copy(), "projT": c(inputs["L0_attn_proj_w"]).T.copy(),
        "wg": c(inputs["L0_ffn_Wg"]), "wu": c(inputs["L0_ffn_Wu"]), "wd": c(inputs["L0_ffn_Wd"]),
        "wq": c(inputs["L0_ffn_Wq"]),
        "keysT": np.ascontiguousarray(np.transpose(np.asarray(inputs["L0_ffn_keys"]), (0, 2, 3, 1))),
        "downT": c(inputs["L0_ffn_down_embed"]).T.copy(), "up": c(inputs["L0_ffn_up_embed"]),
        "qb": c(inputs["L0_attn_q_b"]).reshape(1, -1), "kb": c(inputs["L0_attn_k_b"]).reshape(1, -1),
        "vb": c(inputs["L0_attn_v_b"]).reshape(1, -1), "ob": c(inputs["L0_attn_o_b"]).reshape(1, -1),
        "projb": c(inputs["L0_attn_proj_b"]).reshape(1, -1), "bd": c(inputs["L0_ffn_bd"]).reshape(1, -1),
        "fcb": c(inputs["L0_attn_fc_b"]), "bg": c(inputs["L0_ffn_bg"]), "bu": c(inputs["L0_ffn_bu"]),
        "ln1w": c(inputs["L0_attn_ln1_w"]), "ln1b": c(inputs["L0_attn_ln1_b"]),
        "ln2w": c(inputs["L0_attn_ln2_w"]), "ln2b": c(inputs["L0_attn_ln2_b"]),
    }
    nc = _get_nc()
    in_maps = [dict(w, x=s) for s in shards]
    res = run_bass_kernel_spmd(nc, in_maps, core_ids=list(range(NCORES)))
    out = np.concatenate([r["y"] for r in res.results], axis=0)
    return out.reshape(np.asarray(inputs["L0_attn_x"]).shape).astype(np.float32)


# version 17
# speedup vs baseline: 1.2931x; 1.2931x over previous
import numpy as np
import concourse.bass as bass
import concourse.mybir as mybir
from concourse.bass_utils import run_bass_kernel_spmd

F32 = mybir.dt.float32
BF16 = mybir.dt.bfloat16
AF = mybir.ActivationFunctionType
ALU = mybir.AluOpType
AX = mybir.AxisListType

NCORES = 8
FOLD_KIND = ""
NOPWIN = 8
T = 2048
NT = T // 128
D = 1024
FH = 4
NK = 64
DCD = 4096
NEXP = 4096
EPS = 1e-5
THR_MARGIN = 1.0 - 4e-6

ENGINES = ("pe", "act", "dve", "pool", "sp")


class Buf:
    __slots__ = ("name", "writers", "readers")

    def __init__(self, name):
        self.name = name
        self.writers = []
        self.readers = []


class Prog:
    def __init__(self):
        self.ops = []
        self._fence_deps = None

    def fence(self):
        last = {}
        for i, o in enumerate(self.ops):
            last[(o[0], o[3])] = i
        self._fence_deps = set(last.values())

    def add(self, engine, emit, reads, writes, dma, kind=""):
        dset = set()
        for b in reads:
            dset.update(b.writers)
        for b in writes:
            dset.update(b.writers)
            dset.update(b.readers)
        if self._fence_deps is not None:
            dset.update(self._fence_deps)
        oid = len(self.ops)
        dset.discard(oid)
        self.ops.append((engine, emit, dset, dma, kind))
        for b in reads:
            b.readers.append(oid)
        for b in writes:
            b.writers.append(oid)
            b.readers = []
        return oid

    def fence_buf(self, b):
        b.writers = []
        b.readers = []

    _COST = {"pe": 250, "act": 700, "dve": 1100, "pool": 900, "sp": 400}

    USE_LIST_SCHED = False

    def schedule(self):
        if not self.USE_LIST_SCHED:
            n = len(self.ops)
            return {e: [i for i in range(n) if self.ops[i][0] == e] for e in ENGINES}
        return self._schedule_list()

    def _schedule_list(self):
        n = len(self.ops)
        eng_of = [o[0] for o in self.ops]
        cost = [self._COST[e] for e in eng_of]
        dependents = [[] for _ in range(n)]
        ndeps = [0] * n
        for i, (_, _, deps, _, _) in enumerate(self.ops):
            ndeps[i] = len(deps)
            for d in deps:
                dependents[d].append(i)
        prio = [0] * n
        for i in range(n - 1, -1, -1):
            m = 0
            for j in dependents[i]:
                if prio[j] > m:
                    m = prio[j]
            prio[i] = m + cost[i]
        import heapq
        ready = {e: [] for e in ENGINES}
        ready_at = [0.0] * n
        done_t = [0.0] * n
        remaining = [len(self.ops[i][2]) for i in range(n)]
        for i in range(n):
            if remaining[i] == 0:
                heapq.heappush(ready[eng_of[i]], (-prio[i], i))
        clock = {e: 0.0 for e in ENGINES}
        order = {e: [] for e in ENGINES}
        total = 0
        while total < n:
            best = None
            for e in ENGINES:
                while ready[e]:
                    _, i = ready[e][0]
                    break
                if not ready[e]:
                    continue
                _, i = ready[e][0]
                start = max(clock[e], ready_at[i])
                key = (start, -prio[i])
                if best is None or key < best[0]:
                    best = (key, e)
            if best is None:
                raise RuntimeError("scheduler deadlock")
            _, e = best
            _, i = heapq.heappop(ready[e])
            start = max(clock[e], ready_at[i])
            fin = start + cost[i]
            clock[e] = fin
            done_t[i] = fin
            order[e].append(i)
            total += 1
            for j in dependents[i]:
                if done_t[i] > ready_at[j]:
                    ready_at[j] = done_t[i]
                remaining[j] -= 1
                if remaining[j] == 0:
                    heapq.heappush(ready[eng_of[j]], (-prio[j], j))
        return order

    def emit(self, sems):
        n = len(self.ops)
        eng_of = [o[0] for o in self.ops]
        dma_of = [o[3] for o in self.ops]
        DEEP = {"bn", "max8", "recip", "sel", "reduce"}
        deep = [(o[4] in DEEP) or (o[0] in ("dve", "act") and not o[3]) for o in self.ops]
        needs_inc = [o[3] for o in self.ops]
        for i, (eng, _, deps, dma, _k) in enumerate(self.ops):
            for d in deps:
                if eng_of[d] != eng or self.ops[d][3] or deep[d]:
                    needs_inc[d] = True
        per_engine = self.schedule()
        pos = [0] * n
        for e in ENGINES:
            for k, i in enumerate(per_engine[e]):
                pos[i] = k
        val_after = [0] * n
        cnt = {}
        for e in ENGINES:
            for i in per_engine[e]:
                key = (e, dma_of[i])
                if needs_inc[i]:
                    cnt[key] = cnt.get(key, 0) + (16 if dma_of[i] else 1)
                val_after[i] = cnt.get(key, 0)
        dma_prefix = {e: [0] * (len(per_engine[e]) + 1) for e in ENGINES}
        for e in ENGINES:
            for k, i in enumerate(per_engine[e]):
                dma_prefix[e][k + 1] = dma_prefix[e][k] + (16 if (dma_of[i] and needs_inc[i]) else 0)

        emitted = {}
        nosync = mybir.DependencyInfo(mybir.DependencyClass.NoSync, mybir.DependencyReason.DATA)

        def run_engine(handle, ename):
            waited = {}
            for i in per_engine[ename]:
                eng, emit, deps, dma, _k = self.ops[i]
                need = {}
                same_eng = []
                need_nop = False
                for d in deps:
                    key = (eng_of[d], dma_of[d])
                    if key[0] == ename and not key[1]:
                        same_eng.append(d)
                        if not deep[d]:
                            continue
                        if ename == "dve" and self.ops[d][4] == "":
                            if pos[i] - pos[d] <= NOPWIN:
                                need_nop = True
                            continue
                    v = val_after[d]
                    if v > need.get(key, 0):
                        need[key] = v
                for key in sorted(need):
                    v = need[key]
                    if v > waited.get(key, 0):
                        handle.wait_ge(sems[key], v)
                        waited[key] = v
                if need_nop:
                    handle.engine_nop()
                inst = emit(handle)
                emitted[i] = inst
                if inst is not None:
                    for d in same_eng:
                        pi = emitted.get(d)
                        if pi is not None:
                            inst.ins.add_dependency(pi.ins.name, nosync)
                if needs_inc[i]:
                    if inst is None:
                        raise RuntimeError(f"op {i} on {ename} needs inc but returned None")
                    inst.then_inc(sems[(ename, dma)], 16 if dma else 1)

        return run_engine


def build_nc(dbg=False, max_ops=None):
    nc = bass.Bass("TRN2", target_bir_lowering=False, debug=False, detect_race_conditions=False)

    def din(name, shape, dtype=F32):
        return nc.dram_tensor(name, list(shape), dtype, kind="ExternalInput").ap()

    x_h = din("x", (T, D))
    qwT_h = din("qwT", (D, D))
    kwT_h = din("kwT", (D, D))
    vwT_h = din("vwT", (D, D))
    owT_h = din("owT", (D, D))
    fcT_h = din("fcT", (D, DCD))
    projT_h = din("projT", (DCD, D))
    wg_h = din("wg", (D, DCD))
    wu_h = din("wu", (D, DCD))
    wd_h = din("wd", (DCD, D))
    wq_h = din("wq", (D, 512))
    keysT_h = din("keysT", (FH, 2, NK, NK))
    downT_h = din("downT", (D, NEXP))
    up_h = din("up", (NEXP, D))
    qb_h = din("qb", (1, D))
    kb_h = din("kb", (1, D))
    vb_h = din("vb", (1, D))
    ob_h = din("ob", (1, D))
    projb_h = din("projb", (1, D))
    bd_h = din("bd", (1, D))
    fcb_h = din("fcb", (DCD,))
    bg_h = din("bg", (DCD,))
    bu_h = din("bu", (DCD,))
    ln1w_h = din("ln1w", (D,))
    ln1b_h = din("ln1b", (D,))
    ln2w_h = din("ln2w", (D,))
    ln2b_h = din("ln2b", (D,))
    y_h = nc.dram_tensor("y", [T, D], F32, kind="ExternalOutput").ap()
    wspill_h = nc.dram_tensor("wspill", [T, NEXP], BF16, kind="Internal").ap()
    dbg_h = {}
    if dbg:
        kk = "ExternalOutput" if dbg == 2 else "Internal"
        for nm in ("dbg_attn", "dbg_fc", "dbg_mlp"):
            dbg_h[nm] = nc.dram_tensor(nm, [T, D], F32, kind=kk).ap()
        dbg_h["dbg_f2"] = nc.dram_tensor("dbg_f2", [128, 8 * T], BF16, kind=kk).ap()
        dbg_h["dbg_f2a"] = nc.dram_tensor("dbg_f2a", [128, 8 * T], BF16, kind=kk).ap()
        dbg_h["dbg_hf"] = nc.dram_tensor("dbg_hf", [128, 8 * T], BF16, kind=kk).ap()
        dbg_h["dbg_w"] = nc.dram_tensor("dbg_w", [T, NEXP], BF16, kind=kk).ap()
        for nm in ("t0_pb", "t0_qe", "t0_ke", "t0_ve", "t0_ab", "t0_qkv", "t0_attn", "t0_z1", "t0_xn"):
            dt = F32 if nm in ("t0_z1",) else BF16
            dbg_h[nm] = nc.dram_tensor(nm, [128, 2048], dt, kind=kk).ap()

    ident_bf = nc.alloc_sbuf_tensor("ident_bf", [128, 128], BF16)
    ident_f = nc.alloc_sbuf_tensor("ident_f", [128, 128], F32)
    ones_bf = nc.alloc_sbuf_tensor("ones_bf", [1, 128], BF16)
    iota_t = nc.alloc_sbuf_tensor("iota_t", [128, 128], F32)
    qb_r = nc.alloc_sbuf_tensor("qb_r", [1, D], BF16)
    kb_r = nc.alloc_sbuf_tensor("kb_r", [1, D], BF16)
    vb_r = nc.alloc_sbuf_tensor("vb_r", [1, D], BF16)
    ob_r = nc.alloc_sbuf_tensor("ob_r", [1, D], BF16)
    projb_r = nc.alloc_sbuf_tensor("projb_r", [1, D], BF16)
    bd_r = nc.alloc_sbuf_tensor("bd_r", [1, D], BF16)
    fcb_c = nc.alloc_sbuf_tensor("fcb_c", [128, 32], F32)
    bg_c = nc.alloc_sbuf_tensor("bg_c", [128, 32], F32)
    bu_c = nc.alloc_sbuf_tensor("bu_c", [128, 32], F32)
    ln1w_c = nc.alloc_sbuf_tensor("ln1w_c", [128, 8], F32)
    ln1b_c = nc.alloc_sbuf_tensor("ln1b_c", [128, 8], F32)
    ln2w_c = nc.alloc_sbuf_tensor("ln2w_c", [128, 8], F32)
    ln2b_c = nc.alloc_sbuf_tensor("ln2b_c", [128, 8], F32)
    rows32 = nc.alloc_sbuf_tensor("rows32", [32, 128], F32)
    keysb = nc.alloc_sbuf_tensor("keysb", [128, 8 * NK], BF16)
    stat6 = nc.alloc_sbuf_tensor("stat6", [128, 12], F32)
    mv2 = nc.alloc_sbuf_tensor("mv2", [128, 2], F32)
    lnsc = nc.alloc_sbuf_tensor("lnsc", [128, 16], F32)
    sc_a = nc.alloc_sbuf_tensor("sc_a", [128, 4], F32)
    s8b = nc.alloc_sbuf_tensor("s8b", [128, 64], F32)
    cgrid = nc.alloc_sbuf_tensor("cgrid", [128, 256], F32)
    w8b = nc.alloc_sbuf_tensor("w8b", [128, 32], F32)
    w8s = nc.alloc_sbuf_tensor("w8s", [128, 32], F32)
    e8b = nc.alloc_sbuf_tensor("e8b", [128, 32], F32)
    zb = nc.alloc_sbuf_tensor("zb", [128, 4], F32)
    rzb = nc.alloc_sbuf_tensor("rzb", [128, 4], F32)
    emnb = nc.alloc_sbuf_tensor("emnb", [128, 4], F32)
    scaleb = nc.alloc_sbuf_tensor("scaleb", [128, 4], F32)
    thrb = nc.alloc_sbuf_tensor("thrb", [128, 4], F32)
    thrsb = nc.alloc_sbuf_tensor("thrsb", [128, 4], F32)
    axpre = nc.alloc_sbuf_tensor("axpre", [128, 512], F32)
    negc = nc.alloc_sbuf_tensor("negc", [128, 1], F32)
    epsb = nc.alloc_sbuf_tensor("epsb", [128, 1], F32)

    XO = (229376 - nc.sbuf_bytes_remaining + 31) & ~31
    assert XO + 65536 + 65536 + 32768 + 25600 <= 229376, f"SBUF overflow: XO={XO}"
    xbuf = nc.alloc_sbuf_tensor_at("xbuf", [128, NT * D], F32, offset=XO)
    WO = XO + 65536
    qT_sb = nc.alloc_sbuf_tensor_at("qT_sb", [128, 8 * D], BF16, offset=WO)
    kT_sb = nc.alloc_sbuf_tensor_at("kT_sb", [128, 8 * D], BF16, offset=WO + 16384)
    vT_sb = nc.alloc_sbuf_tensor_at("vT_sb", [128, 8 * D], BF16, offset=WO + 32768)
    oT_sb = nc.alloc_sbuf_tensor_at("oT_sb", [128, 8 * D], BF16, offset=WO + 49152)
    fcT_s = nc.alloc_sbuf_tensor_at("fcT_s", [128, 2 * 1024], BF16, offset=WO)
    projT_s = nc.alloc_sbuf_tensor_at("projT_s", [128, 8 * 1024], BF16, offset=WO + 4096)
    hFg = nc.alloc_sbuf_tensor_at("hFg", [128, 8 * T], BF16, offset=WO + 20480)
    downR = nc.alloc_sbuf_tensor_at("downR", [128, 8 * NEXP], BF16, offset=WO)
    wgu_s = nc.alloc_sbuf_tensor_at("wgu_s", [128, 2 * 2048], BF16, offset=WO)
    wd_s = nc.alloc_sbuf_tensor_at("wd_s", [128, 8 * 1024], BF16, offset=WO + 8192)
    prodg = nc.alloc_sbuf_tensor_at("prodg", [128, 8 * T], BF16, offset=WO + 24576)
    upR = nc.alloc_sbuf_tensor_at("upR", [128, 32 * D], BF16, offset=WO)
    F2O = WO + 65536
    f2 = nc.alloc_sbuf_tensor_at("f2", [128, 8 * T], BF16, offset=F2O)
    AO = F2O + 32768
    xnF_t = nc.alloc_sbuf_tensor_at("xnF_t", [128, D], BF16, offset=AO)
    pbA_t = nc.alloc_sbuf_tensor_at("pbA_t", [128, D], BF16, offset=AO + 2048)
    qe_t = nc.alloc_sbuf_tensor_at("qe_t", [128, D], BF16, offset=AO + 4096)
    ke_t = nc.alloc_sbuf_tensor_at("ke_t", [128, D], BF16, offset=AO + 6144)
    ve_t = nc.alloc_sbuf_tensor_at("ve_t", [128, D], BF16, offset=AO + 8192)
    pb_t = nc.alloc_sbuf_tensor_at("pb_t", [128, 2048], BF16, offset=AO + 10240)
    tmpa_t = nc.alloc_sbuf_tensor_at("tmpa_t", [128, D], F32, offset=AO + 10240)
    attnF_t = nc.alloc_sbuf_tensor_at("attnF_t", [128, D], BF16, offset=AO + 14336)
    ab_t = nc.alloc_sbuf_tensor_at("ab_t", [128, 256], BF16, offset=AO + 16384)
    qk_t = nc.alloc_sbuf_tensor_at("qk_t", [128, 16], F32, offset=AO + 16896)
    rq_t = nc.alloc_sbuf_tensor_at("rq_t", [128, 16], F32, offset=AO + 16960)
    qkv_t = nc.alloc_sbuf_tensor_at("qkv_t", [128, D], BF16, offset=AO + 17024)
    z1_t = nc.alloc_sbuf_tensor_at("z1_t", [128, D], F32, offset=AO + 19072)
    e1E = nc.alloc_sbuf_tensor_at("e1E", [128, NEXP], BF16, offset=AO)
    pgE = nc.alloc_sbuf_tensor_at("pgE", [128, 2048], F32, offset=AO + 8192)
    mskE = nc.alloc_sbuf_tensor_at("mskE", [128, 2048], BF16, offset=AO + 16384)
    wch = nc.alloc_sbuf_tensor_at("wch", [128, 512], BF16, offset=AO + 20480)
    silch = nc.alloc_sbuf_tensor_at("silch", [128, 512], BF16, offset=AO + 21504)
    simsb = nc.alloc_sbuf_tensor_at("simsb", [128, 512], F32, offset=AO + 22528)
    mskb = nc.alloc_sbuf_tensor_at("mskb", [128, 512], mybir.dt.uint8, offset=AO + 24576)
    qq_dram = nc.dram_tensor("qqspill", [4 * 128, T], BF16, kind="Internal").ap()
    wtin = nc.alloc_sbuf_tensor_at("wtin", [128, 2 * NEXP], BF16, offset=AO)
    wtt = nc.alloc_sbuf_tensor_at("wtt", [128, 32 * 128], BF16, offset=AO + 16384)
    wch2 = nc.alloc_sbuf_tensor_at("wch2", [128, 512], BF16, offset=AO + 27136)
    sgch = nc.alloc_sbuf_tensor_at("sgch", [128, 512], BF16, offset=AO + 25088)
    qqst = nc.alloc_sbuf_tensor_at("qqst", [64, 8 * 128], BF16, offset=AO + 25088)

    PA = nc.alloc_psum_tensor("PA", [128, 1024], F32)
    PB = nc.alloc_psum_tensor("PB", [128, 1024], F32)
    PC = nc.alloc_psum_tensor("PC", [128, 1024], F32)
    PT = nc.alloc_psum_tensor("PT", [128, 1024], BF16)
    PS = nc.alloc_psum_tensor("PS", [128, 512], F32)

    B = {}

    def buf(nm):
        if nm not in B:
            B[nm] = Buf(nm)
        return B[nm]

    P = Prog()

    REGION = {}
    for nm in ("qT", "kT", "vT", "oT"):
        REGION[nm] = ("WREG", "att")
    for nm in ("fcw0", "fcw1", "hFg") + tuple(f"pjw{j}" for j in range(8)):
        REGION[nm] = ("WREG", "fc")
    REGION["down"] = ("WREG", "e1w")
    for nm in ("wgw0", "wgw1", "wuw0", "wuw1", "prodg") + tuple(f"wdw{j}" for j in range(8)):
        REGION[nm] = ("WREG", "mlp")
    REGION["upw"] = ("WREG", "e2w")
    for nm in ("xn_t", "pbA", "qe", "ke", "ve", "pb", "attnF", "A", "qk", "rq", "qkv", "z1"):
        REGION[nm] = ("RA", "attn")
    for nm in ("E", "Pg0", "Pg1", "Emask0", "Emask1", "sims", "msk", "wc0", "sil0",
               "qqt", "qqsb", "wqs"):
        REGION[nm] = ("RA", "e1a")
    for nm in ("wt0", "wt1", "wtt"):
        REGION[nm] = ("RA", "e2a")
    REGION["sg"] = ("RA", "mlpa")
    _rstate = {}

    def op(engine, emit, r=(), w=(), dma=False, kind=""):
        extra = []
        for nm in tuple(r) + tuple(w):
            reg = REGION.get(nm)
            if reg is None:
                continue
            region, group = reg
            st = _rstate.setdefault(region, {"group": group, "cur": [], "prev": []})
            if st["group"] != group:
                st["prev"] = st["cur"]
                st["cur"] = []
                st["group"] = group
            extra.extend(st["prev"])
        oid = P.add(engine, emit, [buf(n) for n in r], [buf(n) for n in w], dma, kind)
        eng_, emit_, dset_, dma_, kind_ = P.ops[oid]
        dset_ = set(dset_) | set(extra)
        dset_.discard(oid)
        P.ops[oid] = (eng_, emit_, dset_, dma_, kind_)
        for nm in tuple(r) + tuple(w):
            reg = REGION.get(nm)
            if reg is not None:
                _rstate[reg[0]]["cur"].append(oid)
        return oid

    def fence():
        pass

    def xt(i):
        return xbuf[:, i * D:(i + 1) * D]

    op("pool", lambda g: g.iota(iota_t[:], pattern=[[1, 128]], base=0, channel_multiplier=-1,
                                allow_small_or_imprecise_dtypes=True), w=["iota"])
    op("pool", lambda g: g.memset(ones_bf[:], 1.0), w=["ones"])
    op("pool", lambda g: g.memset(negc[:], -1e9), w=["negc"])
    op("pool", lambda g: g.memset(epsb[:], EPS), w=["epsb"])
    op("dve", lambda v: v.tensor_scalar(ident_bf[:], iota_t[:], 0.0, None, ALU.is_equal),
       r=["iota"], w=["identbf"])
    op("dve", lambda v: v.tensor_scalar(ident_f[:], iota_t[:], 0.0, None, ALU.is_equal),
       r=["iota"], w=["identf"])
    op("sp", lambda s: s.dma_start(out=xbuf.ap().rearrange("p (n d) -> p n d", d=D),
                                   in_=x_h.rearrange("(n p) d -> p n d", p=128)),
       w=["x"], dma=True)
    for nm, hsrc, dst in (("qb", qb_h, qb_r), ("kb", kb_h, kb_r), ("vb", vb_h, vb_r),
                          ("ob", ob_h, ob_r), ("projb", projb_h, projb_r), ("bd", bd_h, bd_r)):
        op("pool", lambda g, hs=hsrc, dd=dst: g.dma_start(out=dd[:], in_=hs[:, :]),
           w=[nm], dma=True)

    def col_build(name, hsrc, nvec, dst):
        op("sp", lambda s: s.dma_start(out=rows32[0:nvec, :],
                                       in_=hsrc.rearrange("(c p) -> c p", p=128)),
           w=["rows32"], dma=True)
        op("pe", lambda t: t.transpose(PS[:, 0:nvec], rows32[0:nvec, :], ident_f[0:nvec, 0:nvec]),
           r=["rows32", "identf"], w=["PS"])
        op("dve", lambda v: v.tensor_copy(dst[:, 0:nvec], PS[:, 0:nvec]),
           r=["PS"], w=[name])

    col_build("fcbc", fcb_h, 32, fcb_c)
    col_build("bgc", bg_h, 32, bg_c)
    col_build("buc", bu_h, 32, bu_c)
    col_build("ln1wc", ln1w_h, 8, ln1w_c)
    col_build("ln1bc", ln1b_h, 8, ln1b_c)
    col_build("ln2wc", ln2w_h, 8, ln2w_c)
    col_build("ln2bc", ln2b_h, 8, ln2b_c)

    op("pool", lambda g: g.dma_start(
        out=keysb[0:64, :].rearrange("n (h p k) -> n h p k", h=4, p=2),
        in_=keysT_h.transpose([2, 0, 1, 3])),
       w=["keys"], dma=True)
    op("pool", lambda g: g.dma_start(
        out=keysb[64:128, :].rearrange("n (h p k) -> n h p k", h=4, p=2),
        in_=keysT_h.transpose([2, 0, 1, 3])),
       w=["keys"], dma=True)
    for nm, hsrc, dst in (("qT", qwT_h, qT_sb), ("kT", kwT_h, kT_sb),
                          ("vT", vwT_h, vT_sb), ("oT", owT_h, oT_sb)):
        op("pool", lambda g, hs=hsrc, dd=dst: g.dma_start(
            out=dd.ap().rearrange("p (c n) -> p c n", n=D),
            in_=hs.rearrange("(c p) n -> p c n", p=128)),
            w=[nm], dma=True)

    def ln_stats(xap, srcbufs, mu_col, r_col, s_col=None):
        op("dve", lambda v: v.bn_stats(stat6[:, 0:6], xap[:, 0:512]), r=srcbufs, w=["stat6"], kind="bn")
        op("dve", lambda v: v.bn_stats(stat6[:, 6:12], xap[:, 512:1024]), r=srcbufs, w=["stat6"], kind="bn")
        op("dve", lambda v: v.bn_aggr(mv2[:], stat6[:]), r=["stat6"], w=["mv2"], kind="bn")
        op("dve", lambda v: v.tensor_copy(lnsc[:, mu_col:mu_col + 1], mv2[:, 0:1]), r=["mv2"], w=["lnsc"])
        op("act", lambda a: a.activation(sc_a[:, 0:1], mv2[:, 1:2], AF.Ln, bias=epsb[:, 0:1]),
           r=["mv2", "epsb"], w=["sca"])
        op("act", lambda a: a.activation(lnsc[:, r_col:r_col + 1], sc_a[:, 0:1], AF.Exp, scale=-0.5),
           r=["sca"], w=["lnsc"])
        if s_col is not None:
            op("dve", lambda v: v.tensor_tensor(sc_a[:, 2:3], lnsc[:, r_col:r_col + 1],
                                                lnsc[:, r_col:r_col + 1], ALU.mult),
               r=["lnsc"], w=["sca"])
            op("dve", lambda v: v.tensor_tensor(sc_a[:, 2:3], sc_a[:, 2:3], mv2[:, 1:2], ALU.mult),
               r=["sca", "mv2"], w=["sca"])
            op("act", lambda a: a.activation(sc_a[:, 3:4], sc_a[:, 2:3], AF.Ln, bias=epsb[:, 0:1]),
               r=["sca", "epsb"], w=["sca"])
            op("act", lambda a: a.activation(sc_a[:, 2:3], sc_a[:, 3:4], AF.Exp, scale=-0.5),
               r=["sca"], w=["sca"])
            op("dve", lambda v: v.tensor_tensor(lnsc[:, s_col:s_col + 1], lnsc[:, r_col:r_col + 1],
                                                sc_a[:, 2:3], ALU.mult), r=["sca", "lnsc"], w=["lnsc"])

    fence()

    def attn_s1(i):
        p = i % 2
        ln_stats(xt(i), ["x"], 0 + p, 2 + p, 4 + p)
        op("dve", lambda v, ii=i, pp=p: v.tensor_scalar(
            xnF_t[:], xt(ii), lnsc[:, 0 + pp:1 + pp], lnsc[:, 4 + pp:5 + pp], ALU.subtract, ALU.mult),
            r=["x", "lnsc"], w=["xn_t"])

    def attn_s2(i):
        for c in range(8):
            op("pe", lambda t, cc=c: t.transpose(
                PT[:, cc * 128:(cc + 1) * 128], xnF_t[:, cc * 128:(cc + 1) * 128], ident_bf[:]),
                r=["xn_t", "identbf"], w=["PT"])
        for c in range(8):
            op("dve", lambda v, cc=c: v.tensor_scalar(
                pbA_t[:, cc * 128:(cc + 1) * 128], PT[:, cc * 128:(cc + 1) * 128],
                ln1w_c[:, cc:cc + 1], ln1b_c[:, cc:cc + 1], ALU.mult, ALU.add),
                r=["PT", "ln1wc", "ln1bc"], w=["pbA"])

        def proj_mm(wsb, wname, brow, bname, pdst, pname):
            for nb in range(2):
                op("pe", lambda t, n=nb: t.matmul(
                    pdst[:, n * 512:(n + 1) * 512], ones_bf[:, :], brow[:, n * 512:(n + 1) * 512],
                    start=True, stop=False), r=["ones", bname], w=[pname])
                for k in range(8):
                    op("pe", lambda t, n=nb, kk=k: t.matmul(
                        pdst[:, n * 512:(n + 1) * 512],
                        pbA_t[:, kk * 128:(kk + 1) * 128],
                        wsb[:, kk * D + n * 512: kk * D + (n + 1) * 512],
                        start=False, stop=(kk == 7)), r=["pbA", wname], w=[pname])

        proj_mm(qT_sb, "qT", qb_r, "qb", PA, "PA")
        proj_mm(kT_sb, "kT", kb_r, "kb", PB, "PB")
        proj_mm(vT_sb, "vT", vb_r, "vb", PC, "PC")

    def attn_s3(i):
        def fold_tree(src_ap, src_n, src_buf, dst_ap, dst_buf, scratch, src_ng):
            gh = src_ng // 2
            cur_ap, cur_buf, d = src_ap, src_buf, src_n
            s = 0
            while d > 1:
                h = d // 2
                if h == 1:
                    out_ap, out_buf = dst_ap, dst_buf
                else:
                    out_ap = scratch[s % len(scratch)][0](h)
                    out_buf = scratch[s % len(scratch)][1]
                for half in range(2):
                    gs = slice(half * gh, (half + 1) * gh)
                    op("dve", lambda v, o=out_ap, a=cur_ap, hh=h, g_=gs: v.tensor_tensor(
                        o[:, g_, :], a[:, g_, 0:hh], a[:, g_, hh:2 * hh], ALU.add),
                       r=[cur_buf], w=[out_buf], kind=FOLD_KIND)
                cur_ap, cur_buf, d = out_ap, out_buf, h
                s += 1

        for hh in range(8):
            in0 = qe_t.ap()[:, hh * 128:(hh + 1) * 128].rearrange("p (a d) -> p a d", a=2) \
                .unsqueeze(2).broadcast_to([128, 2, 16, 64])
            in1 = ke_t.ap().rearrange("p (b d) -> p b d", b=16).unsqueeze(1).broadcast_to([128, 2, 16, 64])
            outp = pb_t.ap().rearrange("p (a b d) -> p a b d", a=2, b=16)
            for ah in range(2):
                op("dve", lambda v, i0=in0, i1=in1, oo=outp, a_=ah: v.tensor_tensor(
                    oo[:, a_:a_ + 1], i0[:, a_:a_ + 1], i1[:, a_:a_ + 1], ALU.mult),
                   r=["qe", "ke"], w=["pb"])
            scr = [(lambda c: attnF_t.ap()[:, 0:32 * c].rearrange("p (g d) -> p g d", g=32), "attnF"),
                   (lambda c: pb_t.ap()[:, 0:32 * c].rearrange("p (g d) -> p g d", g=32), "pb")]
            fold_tree(pb_t.ap().rearrange("p (g d) -> p g d", d=64), 64, "pb",
                      ab_t.ap()[:, hh * 32:(hh + 1) * 32].unsqueeze(2), "A", scr, 32)
        scrq = [(lambda c: attnF_t.ap()[:, 0:16 * c].rearrange("p (g d) -> p g d", g=16), "attnF"),
                (lambda c: pb_t.ap()[:, 0:16 * c].rearrange("p (g d) -> p g d", g=16), "pb")]
        fold_tree(ab_t.ap().rearrange("p (g b) -> p g b", b=16), 16, "A",
                  qk_t.ap().unsqueeze(2), "qk", scrq, 16)
        op("dve", lambda v: v.reciprocal(rq_t[:], qk_t[:]), r=["qk"], w=["rq"], kind="recip")
        for hh in range(8):
            in0 = ab_t.ap()[:, hh * 32:(hh + 1) * 32].rearrange("p (a b) -> p a b", a=2) \
                .unsqueeze(3).broadcast_to([128, 2, 16, 64])
            in1 = ve_t.ap().rearrange("p (b e) -> p b e", b=16).unsqueeze(1).broadcast_to([128, 2, 16, 64])
            outp = pb_t.ap().rearrange("p (a e b) -> p a b e", a=2, e=64, b=16)
            for ah in range(2):
                op("dve", lambda v, i0=in0, i1=in1, oo=outp, a_=ah: v.tensor_tensor(
                    oo[:, a_:a_ + 1], i0[:, a_:a_ + 1], i1[:, a_:a_ + 1], ALU.mult),
                   r=["A", "ve"], w=["pb"])
            scr2 = [(lambda c: attnF_t.ap()[:, 0:128 * c].rearrange("p (g d) -> p g d", g=128), "attnF"),
                    (lambda c: pb_t.ap()[:, 0:128 * c].rearrange("p (g d) -> p g d", g=128), "pb")]
            fold_tree(pb_t.ap().rearrange("p (g b) -> p g b", b=16), 16, "pb",
                      qkv_t.ap()[:, hh * 128:(hh + 1) * 128].unsqueeze(2), "qkv", scr2, 128)
        op("dve", lambda v: v.tensor_tensor(
            qkv_t.ap().rearrange("p (h e) -> p h e", h=16),
            qkv_t.ap().rearrange("p (h e) -> p h e", h=16),
            rq_t.ap().unsqueeze(2).broadcast_to([128, 16, 64]), ALU.mult),
           r=["qkv", "rq"], w=["qkv"])

    def attn_s4(i):
        def elu_evac(psrc, pname, dst, dname):
            op("dve", lambda v: v.tensor_scalar(tmpa_t[:], psrc[:], 0.0, None, ALU.min),
               r=[pname], w=["pb"])
            op("act", lambda a: a.activation(z1_t[:], tmpa_t[:], AF.Exp), r=["pb"], w=["z1"])
            op("dve", lambda v: v.scalar_tensor_tensor(dst[:], psrc[:], 0.0, z1_t[:], ALU.max, ALU.add),
               r=[pname, "z1"], w=[dname])

        elu_evac(PA, "PA", qe_t, "qe")
        elu_evac(PB, "PB", ke_t, "ke")
        op("dve", lambda v: v.tensor_copy(ve_t[:], PC[:]), r=["PC"], w=["ve"])

    def attn_s5(i):
        for c in range(8):
            op("pe", lambda t, cc=c: t.transpose(
                PT[:, cc * 128:(cc + 1) * 128], qkv_t[:, cc * 128:(cc + 1) * 128], ident_bf[:]),
                r=["qkv", "identbf"], w=["PT"])
        for c in range(8):
            op("act", lambda a, cc=c: a.activation(
                attnF_t[:, cc * 128:(cc + 1) * 128], PT[:, cc * 128:(cc + 1) * 128], AF.Copy),
                r=["PT"], w=["attnF"])
        for nb in range(2):
            op("pe", lambda t, n=nb: t.matmul(
                PA[:, n * 512:(n + 1) * 512], ones_bf[:, :], ob_r[:, n * 512:(n + 1) * 512],
                start=True, stop=False), r=["ones", "ob"], w=["PA"])
            for k in range(8):
                op("pe", lambda t, n=nb, kk=k: t.matmul(
                    PA[:, n * 512:(n + 1) * 512],
                    attnF_t[:, kk * 128:(kk + 1) * 128],
                    oT_sb[:, kk * D + n * 512: kk * D + (n + 1) * 512],
                    start=False, stop=(kk == 7)), r=["attnF", "oT"], w=["PA"])

    def attn_s6(i):
        p = i % 2
        op("dve", lambda v, ii=i, pp=p: v.tensor_scalar(
            z1_t[:], xt(ii), lnsc[:, 0 + pp:1 + pp], lnsc[:, 2 + pp:3 + pp], ALU.subtract, ALU.mult),
            r=["x", "lnsc"], w=["z1"])
        op("dve", lambda v: v.tensor_tensor(z1_t[:], z1_t[:], PA[:], ALU.add),
           r=["z1", "PA"], w=["z1"])
        op("dve", lambda v, ii=i: v.tensor_tensor(xt(ii), xt(ii), z1_t[:], ALU.add),
           r=["z1", "x"], w=["x"])
        ln_stats(z1_t, ["z1"], 6 + p, 8 + p)
        op("dve", lambda v, pp=p: v.tensor_scalar(
            attnF_t[:], z1_t[:], lnsc[:, 6 + pp:7 + pp], lnsc[:, 8 + pp:9 + pp], ALU.subtract, ALU.mult),
            r=["z1", "lnsc"], w=["attnF"])
        for c in range(8):
            op("pe", lambda t, cc=c: t.transpose(
                PT[:, cc * 128:(cc + 1) * 128], attnF_t[:, cc * 128:(cc + 1) * 128], ident_bf[:]),
                r=["attnF", "identbf"], w=["PT"])
        for c in range(8):
            op("dve", lambda v, cc=c, ii=i: v.tensor_scalar(
                f2[:, cc * T + ii * 128: cc * T + (ii + 1) * 128],
                PT[:, cc * 128:(cc + 1) * 128],
                ln2w_c[:, cc:cc + 1], ln2b_c[:, cc:cc + 1], ALU.mult, ALU.add),
                r=["PT", "ln2wc", "ln2bc"], w=["f2"])

    for i in range(NT + 1):
        if i < NT:
            attn_s1(i)
            attn_s2(i)
        if i > 0:
            attn_s3(i - 1)
        if i < NT:
            attn_s4(i)
        if i > 0:
            attn_s5(i - 1)
            attn_s6(i - 1)

    fence()
    for grp in range(4):
        for mi in range(8):
            m = grp * 8 + mi
            sl = (m % 2) * 1024
            op("pool", lambda g, mm=m, ss=sl: g.dma_start(
                out=fcT_s[:, ss:ss + 1024].rearrange("p (k m) -> p k m", m=128),
                in_=fcT_h[:, mm * 128:(mm + 1) * 128].rearrange("(k p) m -> p k m", p=128)),
                w=[f"fcw{m % 2}"], dma=True)
            for nb in range(4):
                pdst = PS[:] if nb % 2 == 0 else PA[:, 0:512]
                pnm = "PS" if nb % 2 == 0 else "PA"
                for k in range(8):
                    op("pe", lambda t, ss=sl, n=nb, kk=k, mm=m, pd=pdst, pn=pnm: t.matmul(
                        pd,
                        fcT_s[:, ss + kk * 128: ss + (kk + 1) * 128],
                        f2[:, kk * T + n * 512: kk * T + (n + 1) * 512],
                        start=(kk == 0), stop=(kk == 7)),
                        r=[f"fcw{m % 2}", "f2"], w=[pnm])
                op("act", lambda a, mm=m, ii=mi, n=nb, pd=pdst, pn=pnm: a.activation(
                    hFg[:, ii * T + n * 512: ii * T + (n + 1) * 512], pd, AF.Gelu,
                    bias=fcb_c[:, mm:mm + 1]),
                    r=[pnm, "fcbc"], w=["hFg"])
            op("pool", lambda g, mm=m, ii=mi: g.dma_start(
                out=projT_s[:, ii * 1024:(ii + 1) * 1024],
                in_=projT_h[mm * 128:(mm + 1) * 128, :]),
                w=[f"pjw{mi}"], dma=True)
        if dbg and grp == 0:
            op("sp", lambda s: s.dma_start(out=dbg_h["dbg_hf"][:, :], in_=hFg[:]), r=["hFg"], w=["dbg"], dma=True)
        for ts in range(NT):
            for mi in range(8):
                m = grp * 8 + mi
                for nb in range(2):
                    op("pe", lambda t, ii=mi, n=nb, tt=ts, g2=grp: t.matmul(
                        PB[:, n * 512:(n + 1) * 512],
                        hFg[:, ii * T + tt * 128: ii * T + (tt + 1) * 128],
                        projT_s[:, ii * 1024 + n * 512: ii * 1024 + (n + 1) * 512],
                        start=(ii == 0), stop=(ii == 7 and g2 != 0)),
                        r=["hFg", f"pjw{mi}"], w=["PB"])
            if grp == 0:
                for nb in range(2):
                    op("pe", lambda t, n=nb: t.matmul(
                        PB[:, n * 512:(n + 1) * 512], ones_bf[:, :],
                        projb_r[:, n * 512:(n + 1) * 512], start=False, stop=True),
                        r=["ones", "projb"], w=["PB"])
            op("dve", lambda v, tt=ts: v.tensor_tensor(xt(tt), xt(tt), PB[:], ALU.add),
               r=["PB", "x"], w=["x"])

    if dbg:
        for i in range(NT):
            op("sp", lambda s, ii=i: s.dma_start(out=dbg_h["dbg_fc"][ii * 128:(ii + 1) * 128, :], in_=xt(ii)),
               r=["x"], w=["dbg"], dma=True)
    fence()
    for i in range(NT):
        ln_stats(xt(i), ["x"], 10, 11)
        op("dve", lambda v, ii=i: v.tensor_scalar(
            attnF_t[:], xt(ii), lnsc[:, 10:11], lnsc[:, 11:12], ALU.subtract, ALU.mult),
            r=["x", "lnsc"], w=["attnF"])
        for c in range(8):
            op("pe", lambda t, cc=c: t.transpose(
                PT[:, cc * 128:(cc + 1) * 128], attnF_t[:, cc * 128:(cc + 1) * 128], ident_bf[:]),
                r=["attnF", "identbf"], w=["PT"])
        for c in range(8):
            op("act", lambda a, cc=c, ii=i: a.activation(
                f2[:, cc * T + ii * 128: cc * T + (ii + 1) * 128],
                PT[:, cc * 128:(cc + 1) * 128], AF.Copy),
                r=["PT"], w=["f2"])

    if dbg:
        op("sp", lambda s: s.dma_start(out=dbg_h["dbg_f2"][:, :], in_=f2[:]), r=["f2"], w=["dbg"], dma=True)
    for mc in range(4):
        op("pool", lambda g, m=mc: g.dma_start(
            out=e1E[:, 0:1024].rearrange("p (k m) -> p k m", m=128),
            in_=wq_h[:, m * 128:(m + 1) * 128].rearrange("(k p) m -> p k m", p=128)),
            w=["wqs"], dma=True)
        for nb in range(4):
            pdst = PS[:] if nb % 2 == 0 else PA[:, 0:512]
            pnm = "PS" if nb % 2 == 0 else "PA"
            for k in range(8):
                op("pe", lambda t, n=nb, kk=k, pd=pdst, pn=pnm: t.matmul(
                    pd,
                    e1E[:, kk * 128:(kk + 1) * 128],
                    f2[:, kk * T + n * 512: kk * T + (n + 1) * 512],
                    start=(kk == 0), stop=(kk == 7)),
                    r=["wqs", "f2"], w=[pnm])
            op("dve", lambda v, n=nb, pd=pdst, pn=pnm: v.tensor_copy(mskE[:, n * 512:(n + 1) * 512], pd),
               r=[pnm], w=["qqsb"])
        op("sp", lambda s, m=mc: s.dma_start(
            out=qq_dram[m * 128:(m + 1) * 128, :], in_=mskE[:]),
            r=["qqsb"], w=["qqd"], dma=True)

    fence()
    op("pool", lambda g: g.dma_start(
        out=downR.ap().rearrange("p (c e) -> p c e", e=NEXP),
        in_=downT_h.rearrange("(c p) e -> p c e", p=128)),
        w=["down"], dma=True)
    EBSLOT = [
        (PA[:, 0:512], "PAl", "PA"), (PA[:, 512:1024], "PAh", "PA"),
        (PB[:, 0:512], "PBl", "PB"), (PB[:, 512:1024], "PBh", "PB"),
        (PC[:, 0:512], "PCl", "PC"), (PC[:, 512:1024], "PCh", "PC"),
        (PS[:, 0:512], "PS", "PS"), (PA[:, 0:512], "PAl", "PA"),
    ]
    WCH = (wch, wch2)
    for i in range(NT):
        if i == 0:
            op("sp", lambda s: s.dma_start(
                out=qqst[:, :].rearrange("p (c t) -> p c t", t=128),
                in_=qq_dram[:, 0:128].rearrange("(c p) t -> p c t", p=64)),
                r=["qqd"], w=["qqt"], dma=True)
        for ph in range(8):
            p_, h_ = ph // 4, ph % 4
            kcol = (h_ * 2 + p_) * 64
            op("pe", lambda t, pp=ph, kc=kcol: t.matmul(
                PS[:, pp * 64:(pp + 1) * 64],
                qqst[:, pp * 128:(pp + 1) * 128],
                keysb[0:64, kc:kc + 64],
                start=True, stop=True),
                r=["qqt", "keys"], w=["PS"])
        op("dve", lambda v: v.tensor_copy(simsb[:], PS[:]), r=["PS"], w=["sims"])
        if i + 1 < NT:
            op("sp", lambda s, ii=i + 1: s.dma_start(
                out=qqst[:, :].rearrange("p (c t) -> p c t", t=128),
                in_=qq_dram[:, ii * 128:(ii + 1) * 128].rearrange("(c p) t -> p c t", p=64)),
                r=["qqd"], w=["qqt"], dma=True)
        def eb_matmuls(eb):
            pd, fn, cn = EBSLOT[eb]
            wnames = [fn, cn] if (i == 0 and fn != cn) else [fn]
            for k in range(8):
                op("pe", lambda t, n=eb, kk=k, ii=i, pp=pd: t.matmul(
                    pp,
                    f2[:, kk * T + ii * 128: kk * T + (ii + 1) * 128],
                    downR[:, kk * NEXP + n * 512: kk * NEXP + (n + 1) * 512],
                    start=(kk == 0), stop=(kk == 7)),
                    r=["f2", "down"], w=(wnames if k == 0 else [fn]))

        for eb in range(7):
            eb_matmuls(eb)
        for ph in range(8):
            op("dve", lambda v, pp=ph: v.max(s8b[:, pp * 8:(pp + 1) * 8], simsb[:, pp * 64:(pp + 1) * 64]),
               r=["sims"], w=["s8"], kind="max8")
        for ph in range(8):
            op("dve", lambda v, pp=ph: v.tensor_scalar(
                mskb[:, pp * 64:(pp + 1) * 64], simsb[:, pp * 64:(pp + 1) * 64],
                s8b[:, pp * 8 + 7: pp * 8 + 8], None, ALU.is_ge),
               r=["sims", "s8"], w=["msk"])

        op("dve", lambda v: v.tensor_tensor(
            cgrid.ap().rearrange("p (h a b) -> p h a b", h=4, a=8),
            s8b.ap()[:, 0:32].rearrange("p (h a) -> p h a", h=4).unsqueeze(3).broadcast_to([128, 4, 8, 8]),
            s8b.ap()[:, 32:64].rearrange("p (h b) -> p h b", h=4).unsqueeze(2).broadcast_to([128, 4, 8, 8]),
            ALU.add), r=["s8"], w=["C"])
        for h in range(4):
            op("dve", lambda v, hh=h: v.max(w8b[:, hh * 8:(hh + 1) * 8], cgrid[:, hh * 64:(hh + 1) * 64]),
               r=["C"], w=["w8"], kind="max8")
        op("dve", lambda v: v.tensor_tensor(
            w8s.ap().rearrange("p (h a) -> p h a", h=4),
            w8b.ap().rearrange("p (h a) -> p h a", h=4),
            w8b.ap().rearrange("p (h a) -> p h a", h=4)[:, :, 0:1].broadcast_to([128, 4, 8]),
            ALU.subtract), r=["w8"], w=["w8sb"])
        op("act", lambda a: a.activation(e8b[:], w8s[:], AF.Exp), r=["w8sb"], w=["e8"])
        op("dve", lambda v: v.reduce_sum(zb[:], e8b.ap().rearrange("p (h a) -> p h a", h=4), axis=AX.X),
           r=["e8"], w=["Z"], kind="reduce")
        op("act", lambda a: a.activation(rzb[:], zb[:], AF.Ln), r=["Z"], w=["rz"])
        op("dve", lambda v: v.tensor_tensor(
            scaleb[:], rzb[:],
            w8b.ap().rearrange("p (h a) -> p h a", h=4)[:, :, 0:1].rearrange("p h o -> p (h o)"),
            ALU.add), r=["rz", "w8"], w=["SCALE"])
        op("dve", lambda v: v.tensor_scalar(emnb[:], scaleb[:], -1.0, None, ALU.mult), r=["SCALE"], w=["emn"])

        op("dve", lambda v: v.select(axpre[:], mskb[:], simsb[:],
                                     negc.ap().broadcast_to([128, 512])),
           r=["msk", "sims", "negc"], w=["ef"], kind="sel")
        def grid_A(j):
            hf = j // 8
            h = (j % 8) // 2
            sub = j % 2
            par = j % 2
            pgn, mkn = f"Pg{par}", f"Emask{par}"
            pg = pgE.ap()[:, par * 1024:(par + 1) * 1024]
            mk = mskE.ap()[:, par * 1024:(par + 1) * 1024]
            in0 = axpre.ap()[:, h * 64 + hf * 32 + sub * 16: h * 64 + hf * 32 + (sub + 1) * 16] \
                .unsqueeze(2).broadcast_to([128, 16, 64])
            in1 = axpre.ap()[:, 256 + h * 64: 256 + (h + 1) * 64].unsqueeze(1).broadcast_to([128, 16, 64])
            op("dve", lambda e, i0=in0, i1=in1, pg_=pg: e.tensor_tensor(
                pg_.rearrange("p (a d) -> p a d", d=64), i0, i1, ALU.add),
               r=["ef"], w=[pgn])
            op("dve", lambda e, hh=h, pg_=pg, mk_=mk: e.tensor_scalar(
                mk_, pg_, w8b[:, hh * 8 + 7: hh * 8 + 8], None, ALU.is_ge),
               r=[pgn, "w8"], w=[mkn])
            eq = pgE.ap().bitcast(BF16)[:, par * 2048: par * 2048 + 1024]
            op("act", lambda a, hh=h, pg_=pg, eq_=eq: a.activation(
                eq_, pg_, AF.Exp, bias=emnb[:, hh:hh + 1]),
               r=[pgn, "emn"], w=[pgn])

        def grid_B(j):
            hf = j // 8
            h = (j % 8) // 2
            sub = j % 2
            par = j % 2
            pgn, mkn = f"Pg{par}", f"Emask{par}"
            eq = pgE.ap().bitcast(BF16)[:, par * 2048: par * 2048 + 1024]
            mk = mskE.ap()[:, par * 1024:(par + 1) * 1024]
            eslice = e1E.ap()[:, hf * 2048 + sub * 1024: hf * 2048 + (sub + 1) * 1024]
            op("dve", lambda e, eq_=eq, mk_=mk: e.tensor_tensor(mk_, eq_, mk_, ALU.mult),
               r=[mkn, pgn], w=[mkn])
            if h == 0:
                op("dve", lambda e, es=eslice, mk_=mk: e.tensor_copy(es, mk_),
                   r=[mkn], w=["E"])
            else:
                op("dve", lambda e, es=eslice, mk_=mk: e.tensor_tensor(es, es, mk_, ALU.add),
                   r=[mkn, "E"], w=["E"])

        for j in range(16):
            grid_A(j)
            if j >= 1:
                grid_B(j - 1)
        grid_B(15)
        def eb_evac(eb):
            pd, fn, cn = EBSLOT[eb]
            op("act", lambda a, pp=pd: a.activation(silch[:], pp, AF.Silu),
               r=([fn] if fn == cn else [fn, cn]), w=["sil0"])
            wc = WCH[eb % 2]
            wcn = f"wc{eb % 2}"
            op("dve", lambda v, n=eb, wc_=wc: v.tensor_tensor(
                wc_[:], e1E[:, n * 512:(n + 1) * 512], silch[:], ALU.mult),
               r=["E", "sil0"], w=[wcn])
            op("sp" if eb % 2 == 0 else "pool", lambda s, n=eb, ii=i, wc_=wc: s.dma_start(
                out=wspill_h[ii * 128:(ii + 1) * 128, n * 512:(n + 1) * 512],
                in_=wc_[:]),
                r=[wcn], w=["wsp"], dma=True)

        eb_evac(0)
        eb_matmuls(7)
        for eb in range(1, 8):
            eb_evac(eb)

    fence()
    for grp in range(4):
        for mi in range(8):
            m = grp * 8 + mi
            sl = (m % 2) * 2048
            op("pool", lambda g, mm=m, ss=sl: g.dma_start(
                out=wgu_s[:, ss:ss + 1024].rearrange("p (k m) -> p k m", m=128),
                in_=wg_h[:, mm * 128:(mm + 1) * 128].rearrange("(k p) m -> p k m", p=128)),
                w=[f"wgw{m % 2}"], dma=True)
            op("pool", lambda g, mm=m, ss=sl: g.dma_start(
                out=wgu_s[:, ss + 1024:ss + 2048].rearrange("p (k m) -> p k m", m=128),
                in_=wu_h[:, mm * 128:(mm + 1) * 128].rearrange("(k p) m -> p k m", p=128)),
                w=[f"wuw{m % 2}"], dma=True)
            for nb in range(4):
                gdst = PS[:] if nb % 2 == 0 else PC[:, 0:512]
                gnm = "PS" if nb % 2 == 0 else "PC"
                udst = PA[:, 0:512] if nb % 2 == 0 else PC[:, 512:1024]
                unm = "PA" if nb % 2 == 0 else "PC2"
                for k in range(8):
                    op("pe", lambda t, ss=sl, n=nb, kk=k, mm=m, pd=gdst, pn=gnm: t.matmul(
                        pd,
                        wgu_s[:, ss + kk * 128: ss + (kk + 1) * 128],
                        f2[:, kk * T + n * 512: kk * T + (n + 1) * 512],
                        start=(kk == 0), stop=(kk == 7)),
                        r=[f"wgw{m % 2}", "f2"], w=[gnm])
                for k in range(8):
                    op("pe", lambda t, ss=sl, n=nb, kk=k, mm=m, pd=udst, pn=unm: t.matmul(
                        pd,
                        wgu_s[:, ss + 1024 + kk * 128: ss + 1024 + (kk + 1) * 128],
                        f2[:, kk * T + n * 512: kk * T + (n + 1) * 512],
                        start=(kk == 0), stop=(kk == 7)),
                        r=[f"wuw{m % 2}", "f2"],
                        w=(["PC2", "PC"] if (m == 0 and nb == 1 and k == 0) else [unm]))
                sgd = sgch[:, 0:512]
                op("act", lambda a, mm=m, pd=gdst, pn=gnm: a.activation(
                    sgch[:], pd, AF.Silu, bias=bg_c[:, mm:mm + 1]),
                    r=[gnm, "bgc"], w=["sg"])
                op("dve", lambda v, mm=m, pd=udst, pn=unm: v.tensor_scalar(
                    pd, pd, bu_c[:, mm:mm + 1], None, ALU.add),
                    r=[unm, "buc"], w=[unm])
                op("dve", lambda v, ii=mi, n=nb, pd=udst, pn=unm: v.tensor_tensor(
                    prodg[:, ii * T + n * 512: ii * T + (n + 1) * 512], sgch[:], pd, ALU.mult),
                   r=["sg", unm], w=["prodg"])
            op("pool", lambda g, mm=m, ii=mi: g.dma_start(
                out=wd_s[:, ii * 1024:(ii + 1) * 1024],
                in_=wd_h[mm * 128:(mm + 1) * 128, :]),
                w=[f"wdw{mi}"], dma=True)
        for ts in range(NT):
            for mi in range(8):
                m = grp * 8 + mi
                for nb in range(2):
                    op("pe", lambda t, ii=mi, n=nb, tt=ts, g2=grp: t.matmul(
                        PB[:, n * 512:(n + 1) * 512],
                        prodg[:, ii * T + tt * 128: ii * T + (tt + 1) * 128],
                        wd_s[:, ii * 1024 + n * 512: ii * 1024 + (n + 1) * 512],
                        start=(ii == 0), stop=(ii == 7 and g2 != 0)),
                        r=["prodg", f"wdw{mi}"], w=["PB"])
            if grp == 0:
                for nb in range(2):
                    op("pe", lambda t, n=nb: t.matmul(
                        PB[:, n * 512:(n + 1) * 512], ones_bf[:, :],
                        bd_r[:, n * 512:(n + 1) * 512], start=False, stop=True),
                        r=["ones", "bd"], w=["PB"])
            op("dve", lambda v, tt=ts: v.tensor_tensor(xt(tt), xt(tt), PB[:], ALU.add),
               r=["PB", "x"], w=["x"])

    if dbg:
        for i in range(NT):
            op("sp", lambda s, ii=i: s.dma_start(out=dbg_h["dbg_mlp"][ii * 128:(ii + 1) * 128, :], in_=xt(ii)),
               r=["x"], w=["dbg"], dma=True)
    fence()
    op("pool", lambda g: g.dma_start(
        out=upR.ap().rearrange("p (c d) -> p c d", d=D),
        in_=up_h.rearrange("(c p) d -> p c d", p=128)),
        w=["upw"], dma=True)
    for i in range(NT):
        sl = (i % 2) * NEXP
        op("sp", lambda s, ii=i, ss=sl: s.dma_start(
            out=wtin[:, ss:ss + NEXP], in_=wspill_h[ii * 128:(ii + 1) * 128, :]),
            r=["wsp"], w=[f"wt{i % 2}"], dma=True)
        if dbg:
            op("sp", lambda s, ii=i, ss=sl: s.dma_start(
                out=dbg_h["dbg_w"][ii * 128:(ii + 1) * 128, :], in_=wtin[:, ss:ss + NEXP]),
                r=[f"wt{i % 2}"], w=["dbg"], dma=True)
        for r4 in range(4):
            for c in range(8):
                cc = r4 * 8 + c
                op("pe", lambda t, ch=cc, ss=sl, ii=i: t.transpose(
                    PT[:, (ch % 8) * 128:((ch % 8) + 1) * 128],
                    wtin[:, ss + ch * 128: ss + (ch + 1) * 128], ident_bf[:]),
                    r=[f"wt{i % 2}", "identbf"], w=["PT"])
            op("dve", lambda v, rr=r4: v.tensor_copy(
                wtt[:, rr * 1024:(rr + 1) * 1024], PT[:]), r=["PT"], w=["wtt"])
        updst = PB.ap() if i % 2 == 0 else PC.ap()
        upnm = "PB" if i % 2 == 0 else "PC"
        for nb in range(2):
            for c in range(32):
                op("pe", lambda t, ch=c, n=nb, pd=updst, pn=upnm: t.matmul(
                    pd[:, n * 512:(n + 1) * 512],
                    wtt[:, ch * 128:(ch + 1) * 128],
                    upR[:, ch * D + n * 512: ch * D + (n + 1) * 512],
                    start=(ch == 0), stop=(ch == 31)),
                    r=["wtt", "upw"], w=[upnm])
        op("dve", lambda v, ii=i, pd=updst, pn=upnm: v.tensor_tensor(xt(ii), xt(ii), pd[:], ALU.add),
           r=[upnm, "x"], w=["x"])

    fence()
    for i in range(NT):
        op("sp", lambda s, ii=i: s.dma_start(
            out=y_h[ii * 128:(ii + 1) * 128, :], in_=xt(ii)),
            r=["x"], w=["yout"], dma=True)
    op("pe", lambda t: t.nop(), r=["yout"])
    op("act", lambda a: a.nop(), r=["yout"])
    op("dve", lambda v: v.nop(), r=["yout"])
    op("pool", lambda g: g.nop(), r=["yout"])
    op("sp", lambda s: s.nop(), r=["yout"])

    if max_ops is not None:
        P.ops = P.ops[:max_ops]
        for i in range(len(P.ops)):
            eng, emit, deps, dma = P.ops[i]
            P.ops[i] = (eng, emit, {d for d in deps if d < max_ops}, dma)
    sems = {}
    for e in ENGINES:
        sems[(e, False)] = nc.alloc_semaphore("semc_" + e)
        sems[(e, True)] = nc.alloc_semaphore("semd_" + e)
    runner = P.emit(sems)
    with nc.allow_low_precision(reason="bf16 matmul-class intermediates; reductions accumulate fp32 internally"):
        with nc.Block() as block:
            block.tensor(lambda t: runner(t, "pe"))
            block.scalar(lambda a: runner(a, "act"))
            block.vector(lambda v: runner(v, "dve"))
            block.gpsimd(lambda g: runner(g, "pool"))
            block.sync(lambda s: runner(s, "sp"))
    return nc


_NC_CACHE = {}


def _get_nc():
    if "nc" not in _NC_CACHE:
        _NC_CACHE["nc"] = build_nc()
    return _NC_CACHE["nc"]


def kernel(**inputs):
    xfull = np.ascontiguousarray(np.asarray(inputs["L0_attn_x"]).reshape(-1, D))
    shards = np.split(xfull, NCORES, axis=0)

    def c(a):
        return np.ascontiguousarray(np.asarray(a))

    w = {
        "qwT": c(inputs["L0_attn_q_w"]).T.copy(), "kwT": c(inputs["L0_attn_k_w"]).T.copy(),
        "vwT": c(inputs["L0_attn_v_w"]).T.copy(), "owT": c(inputs["L0_attn_o_w"]).T.copy(),
        "fcT": c(inputs["L0_attn_fc_w"]).T.copy(), "projT": c(inputs["L0_attn_proj_w"]).T.copy(),
        "wg": c(inputs["L0_ffn_Wg"]), "wu": c(inputs["L0_ffn_Wu"]), "wd": c(inputs["L0_ffn_Wd"]),
        "wq": c(inputs["L0_ffn_Wq"]),
        "keysT": np.ascontiguousarray(np.transpose(np.asarray(inputs["L0_ffn_keys"]), (0, 2, 3, 1))),
        "downT": c(inputs["L0_ffn_down_embed"]).T.copy(), "up": c(inputs["L0_ffn_up_embed"]),
        "qb": c(inputs["L0_attn_q_b"]).reshape(1, -1), "kb": c(inputs["L0_attn_k_b"]).reshape(1, -1),
        "vb": c(inputs["L0_attn_v_b"]).reshape(1, -1), "ob": c(inputs["L0_attn_o_b"]).reshape(1, -1),
        "projb": c(inputs["L0_attn_proj_b"]).reshape(1, -1), "bd": c(inputs["L0_ffn_bd"]).reshape(1, -1),
        "fcb": c(inputs["L0_attn_fc_b"]), "bg": c(inputs["L0_ffn_bg"]), "bu": c(inputs["L0_ffn_bu"]),
        "ln1w": c(inputs["L0_attn_ln1_w"]), "ln1b": c(inputs["L0_attn_ln1_b"]),
        "ln2w": c(inputs["L0_attn_ln2_w"]), "ln2b": c(inputs["L0_attn_ln2_b"]),
    }
    nc = _get_nc()
    in_maps = [dict(w, x=s) for s in shards]
    res = run_bass_kernel_spmd(nc, in_maps, core_ids=list(range(NCORES)))
    out = np.concatenate([r["y"] for r in res.results], axis=0)
    return out.reshape(np.asarray(inputs["L0_attn_x"]).shape).astype(np.float32)


# version 20 (final)
# speedup vs baseline: 1.5046x; 1.1635x over previous
import numpy as np
import concourse.bass as bass
import concourse.mybir as mybir
from concourse.bass_utils import run_bass_kernel_spmd

F32 = mybir.dt.float32
BF16 = mybir.dt.bfloat16
AF = mybir.ActivationFunctionType
ALU = mybir.AluOpType
AX = mybir.AxisListType

NCORES = 8
FOLD_KIND = ""
NOPWIN = 8
T = 2048
NT = T // 128
D = 1024
FH = 4
NK = 64
DCD = 4096
NEXP = 4096
EPS = 1e-5
THR_MARGIN = 1.0 - 4e-6

ENGINES = ("pe", "act", "dve", "pool", "sp")


class Buf:
    __slots__ = ("name", "writers", "readers")

    def __init__(self, name):
        self.name = name
        self.writers = []
        self.readers = []


class Prog:
    def __init__(self):
        self.ops = []
        self._fence_deps = None

    def fence(self):
        last = {}
        for i, o in enumerate(self.ops):
            last[(o[0], o[3])] = i
        self._fence_deps = set(last.values())

    def add(self, engine, emit, reads, writes, dma, kind=""):
        dset = set()
        for b in reads:
            dset.update(b.writers)
        for b in writes:
            dset.update(b.writers)
            dset.update(b.readers)
        if self._fence_deps is not None:
            dset.update(self._fence_deps)
        oid = len(self.ops)
        dset.discard(oid)
        self.ops.append((engine, emit, dset, dma, kind))
        for b in reads:
            b.readers.append(oid)
        for b in writes:
            b.writers.append(oid)
            b.readers = []
        return oid

    def fence_buf(self, b):
        b.writers = []
        b.readers = []

    _COST = {"pe": 250, "act": 700, "dve": 1100, "pool": 900, "sp": 400}

    USE_LIST_SCHED = False

    def schedule(self):
        if not self.USE_LIST_SCHED:
            n = len(self.ops)
            return {e: [i for i in range(n) if self.ops[i][0] == e] for e in ENGINES}
        return self._schedule_list()

    def _schedule_list(self):
        n = len(self.ops)
        eng_of = [o[0] for o in self.ops]
        cost = [self._COST[e] for e in eng_of]
        dependents = [[] for _ in range(n)]
        ndeps = [0] * n
        for i, (_, _, deps, _, _) in enumerate(self.ops):
            ndeps[i] = len(deps)
            for d in deps:
                dependents[d].append(i)
        prio = [0] * n
        for i in range(n - 1, -1, -1):
            m = 0
            for j in dependents[i]:
                if prio[j] > m:
                    m = prio[j]
            prio[i] = m + cost[i]
        import heapq
        ready = {e: [] for e in ENGINES}
        ready_at = [0.0] * n
        done_t = [0.0] * n
        remaining = [len(self.ops[i][2]) for i in range(n)]
        for i in range(n):
            if remaining[i] == 0:
                heapq.heappush(ready[eng_of[i]], (-prio[i], i))
        clock = {e: 0.0 for e in ENGINES}
        order = {e: [] for e in ENGINES}
        total = 0
        while total < n:
            best = None
            for e in ENGINES:
                while ready[e]:
                    _, i = ready[e][0]
                    break
                if not ready[e]:
                    continue
                _, i = ready[e][0]
                start = max(clock[e], ready_at[i])
                key = (start, -prio[i])
                if best is None or key < best[0]:
                    best = (key, e)
            if best is None:
                raise RuntimeError("scheduler deadlock")
            _, e = best
            _, i = heapq.heappop(ready[e])
            start = max(clock[e], ready_at[i])
            fin = start + cost[i]
            clock[e] = fin
            done_t[i] = fin
            order[e].append(i)
            total += 1
            for j in dependents[i]:
                if done_t[i] > ready_at[j]:
                    ready_at[j] = done_t[i]
                remaining[j] -= 1
                if remaining[j] == 0:
                    heapq.heappush(ready[eng_of[j]], (-prio[j], j))
        return order

    def emit(self, sems):
        n = len(self.ops)
        eng_of = [o[0] for o in self.ops]
        dma_of = [o[3] for o in self.ops]
        DEEP = {"bn", "max8", "recip", "sel", "reduce"}
        deep = [(o[4] in DEEP) or (o[0] in ("dve", "act") and not o[3]) for o in self.ops]
        needs_inc = [o[3] for o in self.ops]
        for i, (eng, _, deps, dma, _k) in enumerate(self.ops):
            for d in deps:
                if eng_of[d] != eng or self.ops[d][3] or deep[d]:
                    needs_inc[d] = True
        per_engine = self.schedule()
        pos = [0] * n
        for e in ENGINES:
            for k, i in enumerate(per_engine[e]):
                pos[i] = k
        val_after = [0] * n
        cnt = {}
        for e in ENGINES:
            for i in per_engine[e]:
                key = (e, dma_of[i])
                if needs_inc[i]:
                    cnt[key] = cnt.get(key, 0) + (16 if dma_of[i] else 1)
                val_after[i] = cnt.get(key, 0)
        dma_prefix = {e: [0] * (len(per_engine[e]) + 1) for e in ENGINES}
        for e in ENGINES:
            for k, i in enumerate(per_engine[e]):
                dma_prefix[e][k + 1] = dma_prefix[e][k] + (16 if (dma_of[i] and needs_inc[i]) else 0)

        emitted = {}
        nosync = mybir.DependencyInfo(mybir.DependencyClass.NoSync, mybir.DependencyReason.DATA)

        def run_engine(handle, ename):
            waited = {}
            for i in per_engine[ename]:
                eng, emit, deps, dma, _k = self.ops[i]
                need = {}
                same_eng = []
                need_nop = False
                for d in deps:
                    key = (eng_of[d], dma_of[d])
                    if key[0] == ename and not key[1]:
                        same_eng.append(d)
                        if not deep[d]:
                            continue
                        if ename == "dve" and self.ops[d][4] == "":
                            if pos[i] - pos[d] <= NOPWIN:
                                need_nop = True
                            continue
                    v = val_after[d]
                    if v > need.get(key, 0):
                        need[key] = v
                for key in sorted(need):
                    v = need[key]
                    if v > waited.get(key, 0):
                        handle.wait_ge(sems[key], v)
                        waited[key] = v
                if need_nop:
                    handle.engine_nop()
                inst = emit(handle)
                emitted[i] = inst
                if inst is not None:
                    for d in same_eng:
                        pi = emitted.get(d)
                        if pi is not None:
                            inst.ins.add_dependency(pi.ins.name, nosync)
                if needs_inc[i]:
                    if inst is None:
                        raise RuntimeError(f"op {i} on {ename} needs inc but returned None")
                    inst.then_inc(sems[(ename, dma)], 16 if dma else 1)

        return run_engine


def build_nc(dbg=False, max_ops=None):
    nc = bass.Bass("TRN2", target_bir_lowering=False, debug=False, detect_race_conditions=False)

    def din(name, shape, dtype=F32):
        return nc.dram_tensor(name, list(shape), dtype, kind="ExternalInput").ap()

    x_h = din("x", (T, D))
    qwT_h = din("qwT", (D, D))
    kwT_h = din("kwT", (D, D))
    vwT_h = din("vwT", (D, D))
    owT_h = din("owT", (D, D))
    fcT_h = din("fcT", (D, DCD))
    projT_h = din("projT", (DCD, D))
    wg_h = din("wg", (D, DCD))
    wu_h = din("wu", (D, DCD))
    wd_h = din("wd", (DCD, D))
    wq_h = din("wq", (D, 512))
    keysT_h = din("keysT", (FH, 2, NK, NK))
    downT_h = din("downT", (D, NEXP))
    up_h = din("up", (NEXP, D))
    qb_h = din("qb", (1, D))
    kb_h = din("kb", (1, D))
    vb_h = din("vb", (1, D))
    ob_h = din("ob", (1, D))
    projb_h = din("projb", (1, D))
    bd_h = din("bd", (1, D))
    fcb_h = din("fcb", (DCD,))
    bg_h = din("bg", (DCD,))
    bu_h = din("bu", (DCD,))
    ln1w_h = din("ln1w", (D,))
    ln1b_h = din("ln1b", (D,))
    ln2w_h = din("ln2w", (D,))
    ln2b_h = din("ln2b", (D,))
    y_h = nc.dram_tensor("y", [T, D], F32, kind="ExternalOutput").ap()
    wspill_h = nc.dram_tensor("wspill", [T, NEXP], BF16, kind="Internal").ap()
    dbg_h = {}
    if dbg:
        kk = "ExternalOutput" if dbg == 2 else "Internal"
        for nm in ("dbg_attn", "dbg_fc", "dbg_mlp"):
            dbg_h[nm] = nc.dram_tensor(nm, [T, D], F32, kind=kk).ap()
        dbg_h["dbg_f2"] = nc.dram_tensor("dbg_f2", [128, 8 * T], BF16, kind=kk).ap()
        dbg_h["dbg_f2a"] = nc.dram_tensor("dbg_f2a", [128, 8 * T], BF16, kind=kk).ap()
        dbg_h["dbg_hf"] = nc.dram_tensor("dbg_hf", [128, 8 * T], BF16, kind=kk).ap()
        dbg_h["dbg_w"] = nc.dram_tensor("dbg_w", [T, NEXP], BF16, kind=kk).ap()
        for nm in ("t0_pb", "t0_qe", "t0_ke", "t0_ve", "t0_ab", "t0_qkv", "t0_attn", "t0_z1", "t0_xn"):
            dt = F32 if nm in ("t0_z1",) else BF16
            dbg_h[nm] = nc.dram_tensor(nm, [128, 2048], dt, kind=kk).ap()

    ident_bf = nc.alloc_sbuf_tensor("ident_bf", [128, 128], BF16)
    ident_f = nc.alloc_sbuf_tensor("ident_f", [128, 128], F32)
    ones_bf = nc.alloc_sbuf_tensor("ones_bf", [1, 128], BF16)
    iota_t = nc.alloc_sbuf_tensor("iota_t", [128, 128], F32)
    qb_r = nc.alloc_sbuf_tensor("qb_r", [1, D], BF16)
    kb_r = nc.alloc_sbuf_tensor("kb_r", [1, D], BF16)
    vb_r = nc.alloc_sbuf_tensor("vb_r", [1, D], BF16)
    ob_r = nc.alloc_sbuf_tensor("ob_r", [1, D], BF16)
    projb_r = nc.alloc_sbuf_tensor("projb_r", [1, D], BF16)
    bd_r = nc.alloc_sbuf_tensor("bd_r", [1, D], BF16)
    fcb_c = nc.alloc_sbuf_tensor("fcb_c", [128, 32], F32)
    bg_c = nc.alloc_sbuf_tensor("bg_c", [128, 32], F32)
    bu_c = nc.alloc_sbuf_tensor("bu_c", [128, 32], F32)
    ln1w_c = nc.alloc_sbuf_tensor("ln1w_c", [128, 8], F32)
    ln1b_c = nc.alloc_sbuf_tensor("ln1b_c", [128, 8], F32)
    ln2w_c = nc.alloc_sbuf_tensor("ln2w_c", [128, 8], F32)
    ln2b_c = nc.alloc_sbuf_tensor("ln2b_c", [128, 8], F32)
    rows32 = nc.alloc_sbuf_tensor("rows32", [32, 128], F32)
    keysb = nc.alloc_sbuf_tensor("keysb", [128, 8 * NK], BF16)
    stat6 = nc.alloc_sbuf_tensor("stat6", [128, 12], F32)
    mv2 = nc.alloc_sbuf_tensor("mv2", [128, 2], F32)
    lnsc = nc.alloc_sbuf_tensor("lnsc", [128, 16], F32)
    sc_a = nc.alloc_sbuf_tensor("sc_a", [128, 4], F32)
    s8b = nc.alloc_sbuf_tensor("s8b", [128, 64], F32)
    cgrid = nc.alloc_sbuf_tensor("cgrid", [128, 256], F32)
    w8b = nc.alloc_sbuf_tensor("w8b", [128, 32], F32)
    w8s = nc.alloc_sbuf_tensor("w8s", [128, 32], F32)
    e8b = nc.alloc_sbuf_tensor("e8b", [128, 32], F32)
    zb = nc.alloc_sbuf_tensor("zb", [128, 4], F32)
    rzb = nc.alloc_sbuf_tensor("rzb", [128, 4], F32)
    emnb = nc.alloc_sbuf_tensor("emnb", [128, 4], F32)
    scaleb = nc.alloc_sbuf_tensor("scaleb", [128, 4], F32)
    thrb = nc.alloc_sbuf_tensor("thrb", [128, 4], F32)
    thrsb = nc.alloc_sbuf_tensor("thrsb", [128, 4], F32)
    axpre = nc.alloc_sbuf_tensor("axpre", [128, 512], F32)
    negc = nc.alloc_sbuf_tensor("negc", [128, 1], F32)
    epsb = nc.alloc_sbuf_tensor("epsb", [128, 1], F32)

    XO = (229376 - nc.sbuf_bytes_remaining + 31) & ~31
    assert XO + 65536 + 65536 + 32768 + 25600 <= 229376, f"SBUF overflow: XO={XO}"
    xbuf = nc.alloc_sbuf_tensor_at("xbuf", [128, NT * D], F32, offset=XO)
    WO = XO + 65536
    qT_sb = nc.alloc_sbuf_tensor_at("qT_sb", [128, 8 * D], BF16, offset=WO)
    kT_sb = nc.alloc_sbuf_tensor_at("kT_sb", [128, 8 * D], BF16, offset=WO + 16384)
    vT_sb = nc.alloc_sbuf_tensor_at("vT_sb", [128, 8 * D], BF16, offset=WO + 32768)
    oT_sb = nc.alloc_sbuf_tensor_at("oT_sb", [128, 8 * D], BF16, offset=WO + 49152)
    fcT_s = nc.alloc_sbuf_tensor_at("fcT_s", [128, 2 * 1024], BF16, offset=WO)
    projT_s = nc.alloc_sbuf_tensor_at("projT_s", [128, 8 * 1024], BF16, offset=WO + 4096)
    hFg = nc.alloc_sbuf_tensor_at("hFg", [128, 8 * T], BF16, offset=WO + 20480)
    downR = nc.alloc_sbuf_tensor_at("downR", [128, 8 * NEXP], BF16, offset=WO)
    wgu_s = nc.alloc_sbuf_tensor_at("wgu_s", [128, 2 * 2048], BF16, offset=WO)
    wd_s = nc.alloc_sbuf_tensor_at("wd_s", [128, 8 * 1024], BF16, offset=WO + 8192)
    prodg = nc.alloc_sbuf_tensor_at("prodg", [128, 8 * T], BF16, offset=WO + 24576)
    upR = nc.alloc_sbuf_tensor_at("upR", [128, 32 * D], BF16, offset=WO)
    F2O = WO + 65536
    f2 = nc.alloc_sbuf_tensor_at("f2", [128, 8 * T], BF16, offset=F2O)
    AO = F2O + 32768
    xnF_t = nc.alloc_sbuf_tensor_at("xnF_t", [128, D], BF16, offset=AO)
    pbA_t = nc.alloc_sbuf_tensor_at("pbA_t", [128, D], BF16, offset=AO + 2048)
    qe_t = nc.alloc_sbuf_tensor_at("qe_t", [128, D], BF16, offset=AO + 4096)
    ke_t = nc.alloc_sbuf_tensor_at("ke_t", [128, D], BF16, offset=AO + 6144)
    ve_t = nc.alloc_sbuf_tensor_at("ve_t", [128, D], BF16, offset=AO + 8192)
    pb_t = nc.alloc_sbuf_tensor_at("pb_t", [128, 2048], BF16, offset=AO + 10240)
    tmpa_t = nc.alloc_sbuf_tensor_at("tmpa_t", [128, D], F32, offset=AO + 10240)
    attnF_t = nc.alloc_sbuf_tensor_at("attnF_t", [128, D], BF16, offset=AO + 14336)
    ab_t = nc.alloc_sbuf_tensor_at("ab_t", [128, 256], BF16, offset=AO + 16384)
    qk_t = nc.alloc_sbuf_tensor_at("qk_t", [128, 16], F32, offset=AO + 16896)
    rq_t = nc.alloc_sbuf_tensor_at("rq_t", [128, 16], F32, offset=AO + 16960)
    qkv_t = nc.alloc_sbuf_tensor_at("qkv_t", [128, D], BF16, offset=AO + 17024)
    z1_t = nc.alloc_sbuf_tensor_at("z1_t", [128, D], F32, offset=AO + 19072)
    e1E = nc.alloc_sbuf_tensor_at("e1E", [128, NEXP], BF16, offset=AO)
    pgE = nc.alloc_sbuf_tensor_at("pgE", [128, 2048], F32, offset=AO + 8192)
    mskE = nc.alloc_sbuf_tensor_at("mskE", [128, 2048], BF16, offset=AO + 16384)
    wch = nc.alloc_sbuf_tensor_at("wch", [128, 512], BF16, offset=AO + 20480)
    silch = nc.alloc_sbuf_tensor_at("silch", [128, 512], BF16, offset=AO + 21504)
    simsb = nc.alloc_sbuf_tensor_at("simsb", [128, 512], F32, offset=AO + 22528)
    mskb = nc.alloc_sbuf_tensor_at("mskb", [128, 512], mybir.dt.uint8, offset=AO + 24576)
    qq_dram = nc.dram_tensor("qqspill", [4 * 128, T], BF16, kind="Internal").ap()
    wtin = nc.alloc_sbuf_tensor_at("wtin", [128, 2 * NEXP], BF16, offset=AO)
    wtt = nc.alloc_sbuf_tensor_at("wtt", [128, 32 * 128], BF16, offset=AO + 16384)
    wch2 = nc.alloc_sbuf_tensor_at("wch2", [128, 512], BF16, offset=AO + 27136)
    silch2 = nc.alloc_sbuf_tensor_at("silch2", [128, 512], BF16, offset=AO + 22528)
    sgch = nc.alloc_sbuf_tensor_at("sgch", [128, 512], BF16, offset=AO + 25088)
    qqst = nc.alloc_sbuf_tensor_at("qqst", [64, 8 * 128], BF16, offset=AO + 25088)

    PA = nc.alloc_psum_tensor("PA", [128, 1024], F32)
    PB = nc.alloc_psum_tensor("PB", [128, 1024], F32)
    PC = nc.alloc_psum_tensor("PC", [128, 1024], F32)
    PT = nc.alloc_psum_tensor("PT", [128, 1024], BF16)
    PS = nc.alloc_psum_tensor("PS", [128, 512], F32)

    B = {}

    def buf(nm):
        if nm not in B:
            B[nm] = Buf(nm)
        return B[nm]

    P = Prog()

    REGION = {}
    for nm in ("qT", "kT", "vT", "oT"):
        REGION[nm] = ("WREG", "att")
    for nm in ("fcw0", "fcw1", "hFg") + tuple(f"pjw{j}" for j in range(8)):
        REGION[nm] = ("WREG", "fc")
    REGION["down"] = ("WREG", "e1w")
    for nm in ("wgw0", "wgw1", "wuw0", "wuw1", "prodg") + tuple(f"wdw{j}" for j in range(8)):
        REGION[nm] = ("WREG", "mlp")
    REGION["upw"] = ("WREG", "e2w")
    for nm in ("xn_t", "pbA", "qe", "ke", "ve", "pb", "attnF", "A", "qk", "rq", "qkv", "z1"):
        REGION[nm] = ("RA", "attn")
    for nm in ("E", "Pg0", "Pg1", "Pg2", "Pg3", "Emask0", "Emask1", "Emask2", "Emask3",
               "sims", "msk", "wc0", "sil0", "sil1", "qqt", "qqsb", "wqs"):
        REGION[nm] = ("RA", "e1a")
    for nm in ("wt0", "wt1", "wtt"):
        REGION[nm] = ("RA", "e2a")
    REGION["sg"] = ("RA", "mlpa")
    _rstate = {}

    def op(engine, emit, r=(), w=(), dma=False, kind=""):
        extra = []
        for nm in tuple(r) + tuple(w):
            reg = REGION.get(nm)
            if reg is None:
                continue
            region, group = reg
            st = _rstate.setdefault(region, {"group": group, "cur": [], "prev": []})
            if st["group"] != group:
                st["prev"] = st["cur"]
                st["cur"] = []
                st["group"] = group
            extra.extend(st["prev"])
        oid = P.add(engine, emit, [buf(n) for n in r], [buf(n) for n in w], dma, kind)
        eng_, emit_, dset_, dma_, kind_ = P.ops[oid]
        dset_ = set(dset_) | set(extra)
        dset_.discard(oid)
        P.ops[oid] = (eng_, emit_, dset_, dma_, kind_)
        for nm in tuple(r) + tuple(w):
            reg = REGION.get(nm)
            if reg is not None:
                _rstate[reg[0]]["cur"].append(oid)
        return oid

    def fence():
        pass

    def xt(i):
        return xbuf[:, i * D:(i + 1) * D]

    op("pool", lambda g: g.iota(iota_t[:], pattern=[[1, 128]], base=0, channel_multiplier=-1,
                                allow_small_or_imprecise_dtypes=True), w=["iota"])
    op("pool", lambda g: g.memset(ones_bf[:], 1.0), w=["ones"])
    op("pool", lambda g: g.memset(negc[:], -1e9), w=["negc"])
    op("pool", lambda g: g.memset(epsb[:], EPS), w=["epsb"])
    op("dve", lambda v: v.tensor_scalar(ident_bf[:], iota_t[:], 0.0, None, ALU.is_equal),
       r=["iota"], w=["identbf"])
    op("dve", lambda v: v.tensor_scalar(ident_f[:], iota_t[:], 0.0, None, ALU.is_equal),
       r=["iota"], w=["identf"])
    op("sp", lambda s: s.dma_start(out=xbuf.ap().rearrange("p (n d) -> p n d", d=D),
                                   in_=x_h.rearrange("(n p) d -> p n d", p=128)),
       w=["x"], dma=True)
    for nm, hsrc, dst in (("qb", qb_h, qb_r), ("kb", kb_h, kb_r), ("vb", vb_h, vb_r),
                          ("ob", ob_h, ob_r), ("projb", projb_h, projb_r), ("bd", bd_h, bd_r)):
        op("pool", lambda g, hs=hsrc, dd=dst: g.dma_start(out=dd[:], in_=hs[:, :]),
           w=[nm], dma=True)

    def col_build(name, hsrc, nvec, dst):
        op("sp", lambda s: s.dma_start(out=rows32[0:nvec, :],
                                       in_=hsrc.rearrange("(c p) -> c p", p=128)),
           w=["rows32"], dma=True)
        op("pe", lambda t: t.transpose(PS[:, 0:nvec], rows32[0:nvec, :], ident_f[0:nvec, 0:nvec]),
           r=["rows32", "identf"], w=["PS"])
        op("dve", lambda v: v.tensor_copy(dst[:, 0:nvec], PS[:, 0:nvec]),
           r=["PS"], w=[name])

    col_build("fcbc", fcb_h, 32, fcb_c)
    col_build("bgc", bg_h, 32, bg_c)
    col_build("buc", bu_h, 32, bu_c)
    col_build("ln1wc", ln1w_h, 8, ln1w_c)
    col_build("ln1bc", ln1b_h, 8, ln1b_c)
    col_build("ln2wc", ln2w_h, 8, ln2w_c)
    col_build("ln2bc", ln2b_h, 8, ln2b_c)

    op("pool", lambda g: g.dma_start(
        out=keysb[0:64, :].rearrange("n (h p k) -> n h p k", h=4, p=2),
        in_=keysT_h.transpose([2, 0, 1, 3])),
       w=["keys"], dma=True)
    op("pool", lambda g: g.dma_start(
        out=keysb[64:128, :].rearrange("n (h p k) -> n h p k", h=4, p=2),
        in_=keysT_h.transpose([2, 0, 1, 3])),
       w=["keys"], dma=True)
    for nm, hsrc, dst in (("qT", qwT_h, qT_sb), ("kT", kwT_h, kT_sb),
                          ("vT", vwT_h, vT_sb), ("oT", owT_h, oT_sb)):
        op("pool", lambda g, hs=hsrc, dd=dst: g.dma_start(
            out=dd.ap().rearrange("p (c n) -> p c n", n=D),
            in_=hs.rearrange("(c p) n -> p c n", p=128)),
            w=[nm], dma=True)

    def ln_stats(xap, srcbufs, mu_col, r_col, s_col=None):
        op("dve", lambda v: v.bn_stats(stat6[:, 0:6], xap[:, 0:512]), r=srcbufs, w=["stat6"], kind="bn")
        op("dve", lambda v: v.bn_stats(stat6[:, 6:12], xap[:, 512:1024]), r=srcbufs, w=["stat6"], kind="bn")
        op("dve", lambda v: v.bn_aggr(mv2[:], stat6[:]), r=["stat6"], w=["mv2"], kind="bn")
        op("dve", lambda v: v.tensor_copy(lnsc[:, mu_col:mu_col + 1], mv2[:, 0:1]), r=["mv2"], w=["lnsc"])
        op("act", lambda a: a.activation(sc_a[:, 0:1], mv2[:, 1:2], AF.Ln, bias=epsb[:, 0:1]),
           r=["mv2", "epsb"], w=["sca"])
        op("act", lambda a: a.activation(lnsc[:, r_col:r_col + 1], sc_a[:, 0:1], AF.Exp, scale=-0.5),
           r=["sca"], w=["lnsc"])
        if s_col is not None:
            op("dve", lambda v: v.tensor_tensor(sc_a[:, 2:3], lnsc[:, r_col:r_col + 1],
                                                lnsc[:, r_col:r_col + 1], ALU.mult),
               r=["lnsc"], w=["sca"])
            op("dve", lambda v: v.tensor_tensor(sc_a[:, 2:3], sc_a[:, 2:3], mv2[:, 1:2], ALU.mult),
               r=["sca", "mv2"], w=["sca"])
            op("act", lambda a: a.activation(sc_a[:, 3:4], sc_a[:, 2:3], AF.Ln, bias=epsb[:, 0:1]),
               r=["sca", "epsb"], w=["sca"])
            op("act", lambda a: a.activation(sc_a[:, 2:3], sc_a[:, 3:4], AF.Exp, scale=-0.5),
               r=["sca"], w=["sca"])
            op("dve", lambda v: v.tensor_tensor(lnsc[:, s_col:s_col + 1], lnsc[:, r_col:r_col + 1],
                                                sc_a[:, 2:3], ALU.mult), r=["sca", "lnsc"], w=["lnsc"])

    fence()

    def attn_s1(i):
        p = i % 2
        ln_stats(xt(i), ["x"], 0 + p, 2 + p, 4 + p)
        op("dve", lambda v, ii=i, pp=p: v.tensor_scalar(
            xnF_t[:], xt(ii), lnsc[:, 0 + pp:1 + pp], lnsc[:, 4 + pp:5 + pp], ALU.subtract, ALU.mult),
            r=["x", "lnsc"], w=["xn_t"])

    def attn_s2(i):
        for c in range(8):
            op("pe", lambda t, cc=c: t.transpose(
                PT[:, cc * 128:(cc + 1) * 128], xnF_t[:, cc * 128:(cc + 1) * 128], ident_bf[:]),
                r=["xn_t", "identbf"], w=["PT"])
        for c in range(8):
            op("dve", lambda v, cc=c: v.tensor_scalar(
                pbA_t[:, cc * 128:(cc + 1) * 128], PT[:, cc * 128:(cc + 1) * 128],
                ln1w_c[:, cc:cc + 1], ln1b_c[:, cc:cc + 1], ALU.mult, ALU.add),
                r=["PT", "ln1wc", "ln1bc"], w=["pbA"])

        def proj_mm(wsb, wname, brow, bname, pdst, pname):
            for nb in range(2):
                op("pe", lambda t, n=nb: t.matmul(
                    pdst[:, n * 512:(n + 1) * 512], ones_bf[:, :], brow[:, n * 512:(n + 1) * 512],
                    start=True, stop=False), r=["ones", bname], w=[pname])
                for k in range(8):
                    op("pe", lambda t, n=nb, kk=k: t.matmul(
                        pdst[:, n * 512:(n + 1) * 512],
                        pbA_t[:, kk * 128:(kk + 1) * 128],
                        wsb[:, kk * D + n * 512: kk * D + (n + 1) * 512],
                        start=False, stop=(kk == 7)), r=["pbA", wname], w=[pname])

        proj_mm(qT_sb, "qT", qb_r, "qb", PA, "PA")
        proj_mm(kT_sb, "kT", kb_r, "kb", PB, "PB")
        proj_mm(vT_sb, "vT", vb_r, "vb", PC, "PC")

    def attn_s3(i):
        def fold_tree(src_ap, src_n, src_buf, dst_ap, dst_buf, scratch, src_ng):
            gh = src_ng // 2
            cur_ap, cur_buf, d = src_ap, src_buf, src_n
            s = 0
            while d > 1:
                h = d // 2
                if h == 1:
                    out_ap, out_buf = dst_ap, dst_buf
                else:
                    out_ap = scratch[s % len(scratch)][0](h)
                    out_buf = scratch[s % len(scratch)][1]
                for half in range(2):
                    gs = slice(half * gh, (half + 1) * gh)
                    op("dve", lambda v, o=out_ap, a=cur_ap, hh=h, g_=gs: v.tensor_tensor(
                        o[:, g_, :], a[:, g_, 0:hh], a[:, g_, hh:2 * hh], ALU.add),
                       r=[cur_buf], w=[out_buf], kind=FOLD_KIND)
                cur_ap, cur_buf, d = out_ap, out_buf, h
                s += 1

        for hh in range(8):
            in0 = qe_t.ap()[:, hh * 128:(hh + 1) * 128].rearrange("p (a d) -> p a d", a=2) \
                .unsqueeze(2).broadcast_to([128, 2, 16, 64])
            in1 = ke_t.ap().rearrange("p (b d) -> p b d", b=16).unsqueeze(1).broadcast_to([128, 2, 16, 64])
            outp = pb_t.ap().rearrange("p (a b d) -> p a b d", a=2, b=16)
            for ah in range(2):
                op("dve", lambda v, i0=in0, i1=in1, oo=outp, a_=ah: v.tensor_tensor(
                    oo[:, a_:a_ + 1], i0[:, a_:a_ + 1], i1[:, a_:a_ + 1], ALU.mult),
                   r=["qe", "ke"], w=["pb"])
            scr = [(lambda c: attnF_t.ap()[:, 0:32 * c].rearrange("p (g d) -> p g d", g=32), "attnF"),
                   (lambda c: pb_t.ap()[:, 0:32 * c].rearrange("p (g d) -> p g d", g=32), "pb")]
            fold_tree(pb_t.ap().rearrange("p (g d) -> p g d", d=64), 64, "pb",
                      ab_t.ap()[:, hh * 32:(hh + 1) * 32].unsqueeze(2), "A", scr, 32)
        scrq = [(lambda c: attnF_t.ap()[:, 0:16 * c].rearrange("p (g d) -> p g d", g=16), "attnF"),
                (lambda c: pb_t.ap()[:, 0:16 * c].rearrange("p (g d) -> p g d", g=16), "pb")]
        fold_tree(ab_t.ap().rearrange("p (g b) -> p g b", b=16), 16, "A",
                  qk_t.ap().unsqueeze(2), "qk", scrq, 16)
        op("dve", lambda v: v.reciprocal(rq_t[:], qk_t[:]), r=["qk"], w=["rq"], kind="recip")
        for hh in range(8):
            in0 = ab_t.ap()[:, hh * 32:(hh + 1) * 32].rearrange("p (a b) -> p a b", a=2) \
                .unsqueeze(3).broadcast_to([128, 2, 16, 64])
            in1 = ve_t.ap().rearrange("p (b e) -> p b e", b=16).unsqueeze(1).broadcast_to([128, 2, 16, 64])
            outp = pb_t.ap().rearrange("p (a b e) -> p a b e", a=2, b=16)
            for ah in range(2):
                op("dve", lambda v, i0=in0, i1=in1, oo=outp, a_=ah: v.tensor_tensor(
                    oo[:, a_:a_ + 1], i0[:, a_:a_ + 1], i1[:, a_:a_ + 1], ALU.mult),
                   r=["A", "ve"], w=["pb"])
            qchain = [
                (pb_t.ap().rearrange("p (a b e) -> p a b e", a=2, b=16), "pb", 16),
                (attnF_t.ap().rearrange("p (a b e) -> p a b e", a=2, b=8), "attnF", 8),
                (pb_t.ap()[:, 0:512].rearrange("p (a b e) -> p a b e", a=2, b=4), "pb", 4),
                (attnF_t.ap()[:, 0:256].rearrange("p (a b e) -> p a b e", a=2, b=2), "attnF", 2),
                (qkv_t.ap()[:, hh * 128:(hh + 1) * 128].rearrange("p (a e) -> p a e", a=2)
                 .unsqueeze(2), "qkv", 1),
            ]
            for st in range(4):
                src, sbuf, bn = qchain[st]
                dst, dbuf, _ = qchain[st + 1]
                for ah in range(2):
                    op("dve", lambda v, s_=src, d_=dst, b_=bn, a_=ah: v.tensor_tensor(
                        d_[:, a_:a_ + 1], s_[:, a_:a_ + 1, 0:b_ // 2, :],
                        s_[:, a_:a_ + 1, b_ // 2:b_, :], ALU.add),
                       r=[sbuf], w=[dbuf])
        op("dve", lambda v: v.tensor_tensor(
            qkv_t.ap().rearrange("p (h e) -> p h e", h=16),
            qkv_t.ap().rearrange("p (h e) -> p h e", h=16),
            rq_t.ap().unsqueeze(2).broadcast_to([128, 16, 64]), ALU.mult),
           r=["qkv", "rq"], w=["qkv"])

    def attn_s4(i):
        def elu_evac(psrc, pname, dst, dname):
            op("dve", lambda v: v.tensor_scalar(tmpa_t[:], psrc[:], 0.0, None, ALU.min),
               r=[pname], w=["pb"])
            op("act", lambda a: a.activation(z1_t[:], tmpa_t[:], AF.Exp), r=["pb"], w=["z1"])
            op("dve", lambda v: v.scalar_tensor_tensor(dst[:], psrc[:], 0.0, z1_t[:], ALU.max, ALU.add),
               r=[pname, "z1"], w=[dname])

        elu_evac(PA, "PA", qe_t, "qe")
        elu_evac(PB, "PB", ke_t, "ke")
        op("dve", lambda v: v.tensor_copy(ve_t[:], PC[:]), r=["PC"], w=["ve"])

    def attn_s5(i):
        for c in range(8):
            op("pe", lambda t, cc=c: t.transpose(
                PT[:, cc * 128:(cc + 1) * 128], qkv_t[:, cc * 128:(cc + 1) * 128], ident_bf[:]),
                r=["qkv", "identbf"], w=["PT"])
        for c in range(8):
            op("act", lambda a, cc=c: a.activation(
                attnF_t[:, cc * 128:(cc + 1) * 128], PT[:, cc * 128:(cc + 1) * 128], AF.Copy),
                r=["PT"], w=["attnF"])
        for nb in range(2):
            op("pe", lambda t, n=nb: t.matmul(
                PA[:, n * 512:(n + 1) * 512], ones_bf[:, :], ob_r[:, n * 512:(n + 1) * 512],
                start=True, stop=False), r=["ones", "ob"], w=["PA"])
            for k in range(8):
                op("pe", lambda t, n=nb, kk=k: t.matmul(
                    PA[:, n * 512:(n + 1) * 512],
                    attnF_t[:, kk * 128:(kk + 1) * 128],
                    oT_sb[:, kk * D + n * 512: kk * D + (n + 1) * 512],
                    start=False, stop=(kk == 7)), r=["attnF", "oT"], w=["PA"])

    def attn_s6(i):
        p = i % 2
        op("dve", lambda v, ii=i, pp=p: v.tensor_scalar(
            z1_t[:], xt(ii), lnsc[:, 0 + pp:1 + pp], lnsc[:, 2 + pp:3 + pp], ALU.subtract, ALU.mult),
            r=["x", "lnsc"], w=["z1"])
        op("dve", lambda v: v.tensor_tensor(z1_t[:], z1_t[:], PA[:], ALU.add),
           r=["z1", "PA"], w=["z1"])
        op("dve", lambda v, ii=i: v.tensor_tensor(xt(ii), xt(ii), z1_t[:], ALU.add),
           r=["z1", "x"], w=["x"])
        ln_stats(z1_t, ["z1"], 6 + p, 8 + p)
        op("dve", lambda v, pp=p: v.tensor_scalar(
            attnF_t[:], z1_t[:], lnsc[:, 6 + pp:7 + pp], lnsc[:, 8 + pp:9 + pp], ALU.subtract, ALU.mult),
            r=["z1", "lnsc"], w=["attnF"])
        for c in range(8):
            op("pe", lambda t, cc=c: t.transpose(
                PT[:, cc * 128:(cc + 1) * 128], attnF_t[:, cc * 128:(cc + 1) * 128], ident_bf[:]),
                r=["attnF", "identbf"], w=["PT"])
        for c in range(8):
            op("dve", lambda v, cc=c, ii=i: v.tensor_scalar(
                f2[:, cc * T + ii * 128: cc * T + (ii + 1) * 128],
                PT[:, cc * 128:(cc + 1) * 128],
                ln2w_c[:, cc:cc + 1], ln2b_c[:, cc:cc + 1], ALU.mult, ALU.add),
                r=["PT", "ln2wc", "ln2bc"], w=["f2"])

    for i in range(NT + 1):
        if i < NT:
            attn_s1(i)
            attn_s2(i)
        if i > 0:
            attn_s3(i - 1)
        if i < NT:
            attn_s4(i)
        if i > 0:
            attn_s5(i - 1)
            attn_s6(i - 1)

    fence()
    for grp in range(4):
        for mi in range(8):
            m = grp * 8 + mi
            sl = (m % 2) * 1024
            op("pool", lambda g, mm=m, ss=sl: g.dma_start(
                out=fcT_s[:, ss:ss + 1024].rearrange("p (k m) -> p k m", m=128),
                in_=fcT_h[:, mm * 128:(mm + 1) * 128].rearrange("(k p) m -> p k m", p=128)),
                w=[f"fcw{m % 2}"], dma=True)
            for nb in range(4):
                pdst = PS[:] if nb % 2 == 0 else PA[:, 0:512]
                pnm = "PS" if nb % 2 == 0 else "PA"
                for k in range(8):
                    op("pe", lambda t, ss=sl, n=nb, kk=k, mm=m, pd=pdst, pn=pnm: t.matmul(
                        pd,
                        fcT_s[:, ss + kk * 128: ss + (kk + 1) * 128],
                        f2[:, kk * T + n * 512: kk * T + (n + 1) * 512],
                        start=(kk == 0), stop=(kk == 7)),
                        r=[f"fcw{m % 2}", "f2"], w=[pnm])
                op("act", lambda a, mm=m, ii=mi, n=nb, pd=pdst, pn=pnm: a.activation(
                    hFg[:, ii * T + n * 512: ii * T + (n + 1) * 512], pd, AF.Gelu,
                    bias=fcb_c[:, mm:mm + 1]),
                    r=[pnm, "fcbc"], w=["hFg"])
            op("pool", lambda g, mm=m, ii=mi: g.dma_start(
                out=projT_s[:, ii * 1024:(ii + 1) * 1024],
                in_=projT_h[mm * 128:(mm + 1) * 128, :]),
                w=[f"pjw{mi}"], dma=True)
        if dbg and grp == 0:
            op("sp", lambda s: s.dma_start(out=dbg_h["dbg_hf"][:, :], in_=hFg[:]), r=["hFg"], w=["dbg"], dma=True)
        for ts in range(NT):
            for mi in range(8):
                m = grp * 8 + mi
                for nb in range(2):
                    op("pe", lambda t, ii=mi, n=nb, tt=ts, g2=grp: t.matmul(
                        PB[:, n * 512:(n + 1) * 512],
                        hFg[:, ii * T + tt * 128: ii * T + (tt + 1) * 128],
                        projT_s[:, ii * 1024 + n * 512: ii * 1024 + (n + 1) * 512],
                        start=(ii == 0), stop=(ii == 7 and g2 != 0)),
                        r=["hFg", f"pjw{mi}"], w=["PB"])
            if grp == 0:
                for nb in range(2):
                    op("pe", lambda t, n=nb: t.matmul(
                        PB[:, n * 512:(n + 1) * 512], ones_bf[:, :],
                        projb_r[:, n * 512:(n + 1) * 512], start=False, stop=True),
                        r=["ones", "projb"], w=["PB"])
            op("dve", lambda v, tt=ts: v.tensor_tensor(xt(tt), xt(tt), PB[:], ALU.add),
               r=["PB", "x"], w=["x"])

    if dbg:
        for i in range(NT):
            op("sp", lambda s, ii=i: s.dma_start(out=dbg_h["dbg_fc"][ii * 128:(ii + 1) * 128, :], in_=xt(ii)),
               r=["x"], w=["dbg"], dma=True)
    fence()
    for i in range(NT):
        ln_stats(xt(i), ["x"], 10, 11)
        op("dve", lambda v, ii=i: v.tensor_scalar(
            attnF_t[:], xt(ii), lnsc[:, 10:11], lnsc[:, 11:12], ALU.subtract, ALU.mult),
            r=["x", "lnsc"], w=["attnF"])
        for c in range(8):
            op("pe", lambda t, cc=c: t.transpose(
                PT[:, cc * 128:(cc + 1) * 128], attnF_t[:, cc * 128:(cc + 1) * 128], ident_bf[:]),
                r=["attnF", "identbf"], w=["PT"])
        for c in range(8):
            op("act", lambda a, cc=c, ii=i: a.activation(
                f2[:, cc * T + ii * 128: cc * T + (ii + 1) * 128],
                PT[:, cc * 128:(cc + 1) * 128], AF.Copy),
                r=["PT"], w=["f2"])

    if dbg:
        op("sp", lambda s: s.dma_start(out=dbg_h["dbg_f2"][:, :], in_=f2[:]), r=["f2"], w=["dbg"], dma=True)
    for mc in range(4):
        op("pool", lambda g, m=mc: g.dma_start(
            out=e1E[:, 0:1024].rearrange("p (k m) -> p k m", m=128),
            in_=wq_h[:, m * 128:(m + 1) * 128].rearrange("(k p) m -> p k m", p=128)),
            w=["wqs"], dma=True)
        for nb in range(4):
            pdst = PS[:] if nb % 2 == 0 else PA[:, 0:512]
            pnm = "PS" if nb % 2 == 0 else "PA"
            for k in range(8):
                op("pe", lambda t, n=nb, kk=k, pd=pdst, pn=pnm: t.matmul(
                    pd,
                    e1E[:, kk * 128:(kk + 1) * 128],
                    f2[:, kk * T + n * 512: kk * T + (n + 1) * 512],
                    start=(kk == 0), stop=(kk == 7)),
                    r=["wqs", "f2"], w=[pnm])
            op("dve", lambda v, n=nb, pd=pdst, pn=pnm: v.tensor_copy(mskE[:, n * 512:(n + 1) * 512], pd),
               r=[pnm], w=["qqsb"])
        op("sp", lambda s, m=mc: s.dma_start(
            out=qq_dram[m * 128:(m + 1) * 128, :], in_=mskE[:]),
            r=["qqsb"], w=["qqd"], dma=True)

    fence()
    op("pool", lambda g: g.dma_start(
        out=downR.ap().rearrange("p (c e) -> p c e", e=NEXP),
        in_=downT_h.rearrange("(c p) e -> p c e", p=128)),
        w=["down"], dma=True)
    EBSLOT = [
        (PA[:, 0:512], "PAl", "PA"), (PA[:, 512:1024], "PAh", "PA"),
        (PB[:, 0:512], "PBl", "PB"), (PB[:, 512:1024], "PBh", "PB"),
        (PC[:, 0:512], "PCl", "PC"), (PC[:, 512:1024], "PCh", "PC"),
        (PS[:, 0:512], "PS", "PS"), (PA[:, 0:512], "PAl", "PA"),
    ]
    WCH = (wch, wch2)
    for i in range(NT):
        if i == 0:
            op("sp", lambda s: s.dma_start(
                out=qqst[:, :].rearrange("p (c t) -> p c t", t=128),
                in_=qq_dram[:, 0:128].rearrange("(c p) t -> p c t", p=64)),
                r=["qqd"], w=["qqt"], dma=True)
        for ph in range(8):
            p_, h_ = ph // 4, ph % 4
            kcol = (h_ * 2 + p_) * 64
            op("pe", lambda t, pp=ph, kc=kcol: t.matmul(
                PS[:, pp * 64:(pp + 1) * 64],
                qqst[:, pp * 128:(pp + 1) * 128],
                keysb[0:64, kc:kc + 64],
                start=True, stop=True),
                r=["qqt", "keys"], w=["PS"])
        op("dve", lambda v: v.tensor_copy(simsb[:], PS[:]), r=["PS"], w=["sims"])
        if i + 1 < NT:
            op("sp", lambda s, ii=i + 1: s.dma_start(
                out=qqst[:, :].rearrange("p (c t) -> p c t", t=128),
                in_=qq_dram[:, ii * 128:(ii + 1) * 128].rearrange("(c p) t -> p c t", p=64)),
                r=["qqd"], w=["qqt"], dma=True)
        def eb_matmuls(eb):
            pd, fn, cn = EBSLOT[eb]
            wnames = [fn, cn] if (i == 0 and fn != cn) else [fn]
            for k in range(8):
                op("pe", lambda t, n=eb, kk=k, ii=i, pp=pd: t.matmul(
                    pp,
                    f2[:, kk * T + ii * 128: kk * T + (ii + 1) * 128],
                    downR[:, kk * NEXP + n * 512: kk * NEXP + (n + 1) * 512],
                    start=(kk == 0), stop=(kk == 7)),
                    r=["f2", "down"], w=(wnames if k == 0 else [fn]))

        for eb in range(7):
            eb_matmuls(eb)
        for ph in range(8):
            op("dve", lambda v, pp=ph: v.max(s8b[:, pp * 8:(pp + 1) * 8], simsb[:, pp * 64:(pp + 1) * 64]),
               r=["sims"], w=["s8"], kind="max8")
        for ph in range(8):
            op("dve", lambda v, pp=ph: v.tensor_scalar(
                mskb[:, pp * 64:(pp + 1) * 64], simsb[:, pp * 64:(pp + 1) * 64],
                s8b[:, pp * 8 + 7: pp * 8 + 8], None, ALU.is_ge),
               r=["sims", "s8"], w=["msk"])

        op("dve", lambda v: v.tensor_tensor(
            cgrid.ap().rearrange("p (h a b) -> p h a b", h=4, a=8),
            s8b.ap()[:, 0:32].rearrange("p (h a) -> p h a", h=4).unsqueeze(3).broadcast_to([128, 4, 8, 8]),
            s8b.ap()[:, 32:64].rearrange("p (h b) -> p h b", h=4).unsqueeze(2).broadcast_to([128, 4, 8, 8]),
            ALU.add), r=["s8"], w=["C"])
        for h in range(4):
            op("dve", lambda v, hh=h: v.max(w8b[:, hh * 8:(hh + 1) * 8], cgrid[:, hh * 64:(hh + 1) * 64]),
               r=["C"], w=["w8"], kind="max8")
        op("dve", lambda v: v.tensor_tensor(
            w8s.ap().rearrange("p (h a) -> p h a", h=4),
            w8b.ap().rearrange("p (h a) -> p h a", h=4),
            w8b.ap().rearrange("p (h a) -> p h a", h=4)[:, :, 0:1].broadcast_to([128, 4, 8]),
            ALU.subtract), r=["w8"], w=["w8sb"])
        op("act", lambda a: a.activation(e8b[:], w8s[:], AF.Exp), r=["w8sb"], w=["e8"])
        op("dve", lambda v: v.reduce_sum(zb[:], e8b.ap().rearrange("p (h a) -> p h a", h=4), axis=AX.X),
           r=["e8"], w=["Z"], kind="reduce")
        op("act", lambda a: a.activation(rzb[:], zb[:], AF.Ln), r=["Z"], w=["rz"])
        op("dve", lambda v: v.tensor_tensor(
            scaleb[:], rzb[:],
            w8b.ap().rearrange("p (h a) -> p h a", h=4)[:, :, 0:1].rearrange("p h o -> p (h o)"),
            ALU.add), r=["rz", "w8"], w=["SCALE"])
        op("dve", lambda v: v.tensor_scalar(emnb[:], scaleb[:], -1.0, None, ALU.mult), r=["SCALE"], w=["emn"])

        op("dve", lambda v: v.select(axpre[:], mskb[:], simsb[:],
                                     negc.ap().broadcast_to([128, 512])),
           r=["msk", "sims", "negc"], w=["ef"], kind="sel")
        def grid_A(j):
            hf = j // 8
            h = (j % 8) // 2
            sub = j % 2
            par = j % 2
            pgn, mkn = f"Pg{par}", f"Emask{par}"
            pg = pgE.ap()[:, par * 1024:(par + 1) * 1024]
            mk = mskE.ap()[:, par * 1024:(par + 1) * 1024]
            in0 = axpre.ap()[:, h * 64 + hf * 32 + sub * 16: h * 64 + hf * 32 + (sub + 1) * 16] \
                .unsqueeze(2).broadcast_to([128, 16, 64])
            in1 = axpre.ap()[:, 256 + h * 64: 256 + (h + 1) * 64].unsqueeze(1).broadcast_to([128, 16, 64])
            op("dve", lambda e, i0=in0, i1=in1, pg_=pg: e.tensor_tensor(
                pg_.rearrange("p (a d) -> p a d", d=64), i0, i1, ALU.add),
               r=["ef"], w=[pgn])
            op("dve", lambda e, hh=h, pg_=pg, mk_=mk: e.tensor_scalar(
                mk_, pg_, w8b[:, hh * 8 + 7: hh * 8 + 8], None, ALU.is_ge),
               r=[pgn, "w8"], w=[mkn])
            eq = pgE.ap().bitcast(BF16)[:, par * 2048: par * 2048 + 1024]
            op("act", lambda a, hh=h, pg_=pg, eq_=eq: a.activation(
                eq_, pg_, AF.Exp, bias=emnb[:, hh:hh + 1]),
               r=[pgn, "emn"], w=[pgn])

        def grid_B(j):
            hf = j // 8
            h = (j % 8) // 2
            sub = j % 2
            par = j % 2
            pgn, mkn = f"Pg{par}", f"Emask{par}"
            eq = pgE.ap().bitcast(BF16)[:, par * 2048: par * 2048 + 1024]
            mk = mskE.ap()[:, par * 1024:(par + 1) * 1024]
            eslice = e1E.ap()[:, hf * 2048 + sub * 1024: hf * 2048 + (sub + 1) * 1024]
            op("dve", lambda e, eq_=eq, mk_=mk: e.tensor_tensor(mk_, eq_, mk_, ALU.mult),
               r=[mkn, pgn], w=[mkn])
            if h == 0:
                op("dve", lambda e, es=eslice, mk_=mk: e.tensor_copy(es, mk_),
                   r=[mkn], w=["E"])
            else:
                op("dve", lambda e, es=eslice, mk_=mk: e.tensor_tensor(es, es, mk_, ALU.add),
                   r=[mkn, "E"], w=["E"])

        for j in range(16):
            grid_A(j)
            if j >= 1:
                grid_B(j - 1)
        grid_B(15)
        def eb_evac(eb):
            pd, fn, cn = EBSLOT[eb]
            sil = silch if eb % 2 == 0 else silch2
            sn = "sil0" if eb % 2 == 0 else "sil1"
            swr = [sn] if eb % 2 == 0 else [sn, "sims"]
            op("act", lambda a, pp=pd, s_=sil: a.activation(s_[:], pp, AF.Silu),
               r=([fn] if fn == cn else [fn, cn]), w=swr)
            wc = WCH[eb % 2]
            wcn = f"wc{eb % 2}"
            op("dve", lambda v, n=eb, wc_=wc, s_=sil: v.tensor_tensor(
                wc_[:], e1E[:, n * 512:(n + 1) * 512], s_[:], ALU.mult),
               r=(["E", sn] if eb % 2 == 0 else ["E", sn, "sims"]), w=[wcn])
            op("sp" if eb % 2 == 0 else "pool", lambda s, n=eb, ii=i, wc_=wc: s.dma_start(
                out=wspill_h[ii * 128:(ii + 1) * 128, n * 512:(n + 1) * 512],
                in_=wc_[:]),
                r=[wcn], w=["wsp"], dma=True)

        eb_evac(0)
        eb_matmuls(7)
        for eb in range(1, 8):
            eb_evac(eb)

    fence()
    for grp in range(4):
        for mi in range(8):
            m = grp * 8 + mi
            sl = (m % 2) * 2048
            op("pool", lambda g, mm=m, ss=sl: g.dma_start(
                out=wgu_s[:, ss:ss + 1024].rearrange("p (k m) -> p k m", m=128),
                in_=wg_h[:, mm * 128:(mm + 1) * 128].rearrange("(k p) m -> p k m", p=128)),
                w=[f"wgw{m % 2}"], dma=True)
            op("pool", lambda g, mm=m, ss=sl: g.dma_start(
                out=wgu_s[:, ss + 1024:ss + 2048].rearrange("p (k m) -> p k m", m=128),
                in_=wu_h[:, mm * 128:(mm + 1) * 128].rearrange("(k p) m -> p k m", p=128)),
                w=[f"wuw{m % 2}"], dma=True)
            for nb in range(4):
                gdst = PS[:] if nb % 2 == 0 else PC[:, 0:512]
                gnm = "PS" if nb % 2 == 0 else "PC"
                udst = PA[:, 0:512] if nb % 2 == 0 else PC[:, 512:1024]
                unm = "PA" if nb % 2 == 0 else "PC2"
                for k in range(8):
                    op("pe", lambda t, ss=sl, n=nb, kk=k, mm=m, pd=gdst, pn=gnm: t.matmul(
                        pd,
                        wgu_s[:, ss + kk * 128: ss + (kk + 1) * 128],
                        f2[:, kk * T + n * 512: kk * T + (n + 1) * 512],
                        start=(kk == 0), stop=(kk == 7)),
                        r=[f"wgw{m % 2}", "f2"], w=[gnm])
                for k in range(8):
                    op("pe", lambda t, ss=sl, n=nb, kk=k, mm=m, pd=udst, pn=unm: t.matmul(
                        pd,
                        wgu_s[:, ss + 1024 + kk * 128: ss + 1024 + (kk + 1) * 128],
                        f2[:, kk * T + n * 512: kk * T + (n + 1) * 512],
                        start=(kk == 0), stop=(kk == 7)),
                        r=[f"wuw{m % 2}", "f2"],
                        w=(["PC2", "PC"] if (m == 0 and nb == 1 and k == 0) else [unm]))
                sgd = sgch[:, 0:512]
                op("act", lambda a, mm=m, pd=gdst, pn=gnm: a.activation(
                    sgch[:], pd, AF.Silu, bias=bg_c[:, mm:mm + 1]),
                    r=[gnm, "bgc"], w=["sg"])
                op("dve", lambda v, mm=m, pd=udst, pn=unm: v.tensor_scalar(
                    pd, pd, bu_c[:, mm:mm + 1], None, ALU.add),
                    r=[unm, "buc"], w=[unm])
                op("dve", lambda v, ii=mi, n=nb, pd=udst, pn=unm: v.tensor_tensor(
                    prodg[:, ii * T + n * 512: ii * T + (n + 1) * 512], sgch[:], pd, ALU.mult),
                   r=["sg", unm], w=["prodg"])
            op("pool", lambda g, mm=m, ii=mi: g.dma_start(
                out=wd_s[:, ii * 1024:(ii + 1) * 1024],
                in_=wd_h[mm * 128:(mm + 1) * 128, :]),
                w=[f"wdw{mi}"], dma=True)
        for ts in range(NT):
            for mi in range(8):
                m = grp * 8 + mi
                for nb in range(2):
                    op("pe", lambda t, ii=mi, n=nb, tt=ts, g2=grp: t.matmul(
                        PB[:, n * 512:(n + 1) * 512],
                        prodg[:, ii * T + tt * 128: ii * T + (tt + 1) * 128],
                        wd_s[:, ii * 1024 + n * 512: ii * 1024 + (n + 1) * 512],
                        start=(ii == 0), stop=(ii == 7 and g2 != 0)),
                        r=["prodg", f"wdw{mi}"], w=["PB"])
            if grp == 0:
                for nb in range(2):
                    op("pe", lambda t, n=nb: t.matmul(
                        PB[:, n * 512:(n + 1) * 512], ones_bf[:, :],
                        bd_r[:, n * 512:(n + 1) * 512], start=False, stop=True),
                        r=["ones", "bd"], w=["PB"])
            op("dve", lambda v, tt=ts: v.tensor_tensor(xt(tt), xt(tt), PB[:], ALU.add),
               r=["PB", "x"], w=["x"])

    if dbg:
        for i in range(NT):
            op("sp", lambda s, ii=i: s.dma_start(out=dbg_h["dbg_mlp"][ii * 128:(ii + 1) * 128, :], in_=xt(ii)),
               r=["x"], w=["dbg"], dma=True)
    fence()
    op("pool", lambda g: g.dma_start(
        out=upR.ap().rearrange("p (c d) -> p c d", d=D),
        in_=up_h.rearrange("(c p) d -> p c d", p=128)),
        w=["upw"], dma=True)
    for i in range(NT):
        sl = (i % 2) * NEXP
        op("sp", lambda s, ii=i, ss=sl: s.dma_start(
            out=wtin[:, ss:ss + NEXP], in_=wspill_h[ii * 128:(ii + 1) * 128, :]),
            r=["wsp"], w=[f"wt{i % 2}"], dma=True)
        if dbg:
            op("sp", lambda s, ii=i, ss=sl: s.dma_start(
                out=dbg_h["dbg_w"][ii * 128:(ii + 1) * 128, :], in_=wtin[:, ss:ss + NEXP]),
                r=[f"wt{i % 2}"], w=["dbg"], dma=True)
        for r4 in range(4):
            for c in range(8):
                cc = r4 * 8 + c
                op("pe", lambda t, ch=cc, ss=sl, ii=i: t.transpose(
                    PT[:, (ch % 8) * 128:((ch % 8) + 1) * 128],
                    wtin[:, ss + ch * 128: ss + (ch + 1) * 128], ident_bf[:]),
                    r=[f"wt{i % 2}", "identbf"], w=["PT"])
            op("dve", lambda v, rr=r4: v.tensor_copy(
                wtt[:, rr * 1024:(rr + 1) * 1024], PT[:]), r=["PT"], w=["wtt"])
        updst = PB.ap() if i % 2 == 0 else PC.ap()
        upnm = "PB" if i % 2 == 0 else "PC"
        for nb in range(2):
            for c in range(32):
                op("pe", lambda t, ch=c, n=nb, pd=updst, pn=upnm: t.matmul(
                    pd[:, n * 512:(n + 1) * 512],
                    wtt[:, ch * 128:(ch + 1) * 128],
                    upR[:, ch * D + n * 512: ch * D + (n + 1) * 512],
                    start=(ch == 0), stop=(ch == 31)),
                    r=["wtt", "upw"], w=[upnm])
        op("dve", lambda v, ii=i, pd=updst, pn=upnm: v.tensor_tensor(xt(ii), xt(ii), pd[:], ALU.add),
           r=[upnm, "x"], w=["x"])

    fence()
    for i in range(NT):
        op("sp", lambda s, ii=i: s.dma_start(
            out=y_h[ii * 128:(ii + 1) * 128, :], in_=xt(ii)),
            r=["x"], w=["yout"], dma=True)
    op("pe", lambda t: t.nop(), r=["yout"])
    op("act", lambda a: a.nop(), r=["yout"])
    op("dve", lambda v: v.nop(), r=["yout"])
    op("pool", lambda g: g.nop(), r=["yout"])
    op("sp", lambda s: s.nop(), r=["yout"])

    if max_ops is not None:
        P.ops = P.ops[:max_ops]
        for i in range(len(P.ops)):
            eng, emit, deps, dma = P.ops[i]
            P.ops[i] = (eng, emit, {d for d in deps if d < max_ops}, dma)
    sems = {}
    for e in ENGINES:
        sems[(e, False)] = nc.alloc_semaphore("semc_" + e)
        sems[(e, True)] = nc.alloc_semaphore("semd_" + e)
    runner = P.emit(sems)
    with nc.allow_low_precision(reason="bf16 matmul-class intermediates; reductions accumulate fp32 internally"):
        with nc.Block() as block:
            block.tensor(lambda t: runner(t, "pe"))
            block.scalar(lambda a: runner(a, "act"))
            block.vector(lambda v: runner(v, "dve"))
            block.gpsimd(lambda g: runner(g, "pool"))
            block.sync(lambda s: runner(s, "sp"))
    return nc


_NC_CACHE = {}


def _get_nc():
    if "nc" not in _NC_CACHE:
        _NC_CACHE["nc"] = build_nc()
    return _NC_CACHE["nc"]


def kernel(**inputs):
    xfull = np.ascontiguousarray(np.asarray(inputs["L0_attn_x"]).reshape(-1, D))
    shards = np.split(xfull, NCORES, axis=0)

    def c(a):
        return np.ascontiguousarray(np.asarray(a))

    w = {
        "qwT": c(inputs["L0_attn_q_w"]).T.copy(), "kwT": c(inputs["L0_attn_k_w"]).T.copy(),
        "vwT": c(inputs["L0_attn_v_w"]).T.copy(), "owT": c(inputs["L0_attn_o_w"]).T.copy(),
        "fcT": c(inputs["L0_attn_fc_w"]).T.copy(), "projT": c(inputs["L0_attn_proj_w"]).T.copy(),
        "wg": c(inputs["L0_ffn_Wg"]), "wu": c(inputs["L0_ffn_Wu"]), "wd": c(inputs["L0_ffn_Wd"]),
        "wq": c(inputs["L0_ffn_Wq"]),
        "keysT": np.ascontiguousarray(np.transpose(np.asarray(inputs["L0_ffn_keys"]), (0, 2, 3, 1))),
        "downT": c(inputs["L0_ffn_down_embed"]).T.copy(), "up": c(inputs["L0_ffn_up_embed"]),
        "qb": c(inputs["L0_attn_q_b"]).reshape(1, -1), "kb": c(inputs["L0_attn_k_b"]).reshape(1, -1),
        "vb": c(inputs["L0_attn_v_b"]).reshape(1, -1), "ob": c(inputs["L0_attn_o_b"]).reshape(1, -1),
        "projb": c(inputs["L0_attn_proj_b"]).reshape(1, -1), "bd": c(inputs["L0_ffn_bd"]).reshape(1, -1),
        "fcb": c(inputs["L0_attn_fc_b"]), "bg": c(inputs["L0_ffn_bg"]), "bu": c(inputs["L0_ffn_bu"]),
        "ln1w": c(inputs["L0_attn_ln1_w"]), "ln1b": c(inputs["L0_attn_ln1_b"]),
        "ln2w": c(inputs["L0_attn_ln2_w"]), "ln2b": c(inputs["L0_attn_ln2_b"]),
    }
    nc = _get_nc()
    in_maps = [dict(w, x=s) for s in shards]
    res = run_bass_kernel_spmd(nc, in_maps, core_ids=list(range(NCORES)))
    out = np.concatenate([r["y"] for r in res.results], axis=0)
    return out.reshape(np.asarray(inputs["L0_attn_x"]).shape).astype(np.float32)
